# version 06
# baseline (speedup 1.0000x reference)
import numpy as np
import ml_dtypes
from contextlib import ExitStack
import concourse.bass as bass
import concourse.mybir as mybir

F32 = mybir.dt.float32
BF16 = mybir.dt.bfloat16
AF = mybir.ActivationFunctionType
ALU = mybir.AluOpType
AX = mybir.AxisListType

T, TH, D, KD = 2048, 1024, 4096, 32
NT, NTH, NH, Q, NCH = 16, 8, 64, 128, 8
DSSM, PROJ, EPS = 8192, 16704, 1e-5

nbf = lambda a: np.asarray(a, dtype=ml_dtypes.bfloat16)


def host_arrays(inp, core):
    x = np.ascontiguousarray(np.asarray(inp["L0_attn_hidden_states"], dtype=np.float32)[core])
    conv_w = np.asarray(inp["L0_attn_conv_w"], dtype=np.float32)
    d = {
        "x": x,
        "W_in": np.ascontiguousarray(np.asarray(inp["L0_attn_W_in"], dtype=np.float32)),
        "W_out": np.ascontiguousarray(np.asarray(inp["L0_attn_W_out"], dtype=np.float32)),
        "conv_pk": np.ascontiguousarray(
            conv_w.reshape(66, 128, 4).transpose(1, 0, 2).reshape(128, 264)),
        "dt_bias_c": np.ascontiguousarray(np.asarray(inp["L0_attn_dt_bias"], dtype=np.float32).reshape(64, 1)),
        "A_log_c": np.ascontiguousarray(np.asarray(inp["L0_attn_A_log"], dtype=np.float32).reshape(64, 1)),
        "D_row": np.ascontiguousarray(np.asarray(inp["L0_attn_D_param"], dtype=np.float32).reshape(1, 64)),
        "nw_rows": np.ascontiguousarray(np.asarray(inp["L0_attn_norm_weight"], dtype=np.float32).reshape(64, 128)),
        "A_w": np.ascontiguousarray(np.asarray(inp["L0_ffn_A_w"], dtype=np.float32)),
        "B_w": np.ascontiguousarray(np.asarray(inp["L0_ffn_B_w"], dtype=np.float32)),
        "Wq_w": np.ascontiguousarray(np.asarray(inp["L0_ffn_Wq_w"], dtype=np.float32)),
        "keys": np.ascontiguousarray(np.asarray(inp["L0_ffn_keys"], dtype=np.float32)),
        "IDbf": nbf(np.eye(128, dtype=np.float32)),
        "IDf32": np.eye(128, dtype=np.float32),
        "I64f": np.eye(64, dtype=np.float32),
        "I16bf": nbf(np.eye(16, dtype=np.float32)),
        "SLT": np.tril(np.ones((128, 128), dtype=np.float32), -1),
        "MASKNEG": np.tril(np.ones((128, 128), dtype=np.float32), -1) * np.float32(-1e30),
        "UT025": np.triu(np.ones((128, 128), dtype=np.float32), 0) * np.float32(0.25),
        "ONEROW127": np.zeros((128, 128), dtype=np.float32),
        "SCANMASK": np.ones((64, TH), dtype=np.float32),
        "EXP8": nbf(np.ascontiguousarray(
            (np.arange(512)[None, :] // 8 == np.arange(64)[:, None]).astype(np.float32))),
    }
    d["ONEROW127"][127, :] = 1.0
    d["SCANMASK"][:, 0::Q] = 0.0
    return d


ENGS = ("t", "v", "s", "y", "g")


class Em:
    def __init__(self, nc, es):
        self.nc = nc
        self.es = es
        self.ops = {e: [] for e in ENGS}
        self.grp = -1
        self.sem = {}
        self.cnt = {}
        self.waited = {e: {} for e in ENGS}
        self.last_v = None
        self.new_group()

    def new_group(self):
        self.grp += 1
        self.last_v = None
        for e in ENGS:
            h = self.es.enter_context(self.nc.semaphore(f"tok_{e}{self.grp}"))
            self.sem[e] = h
            self.cnt[e] = 0
        self.dsem = {}
        self.dcnt = {}

    def _dma_sem(self, name):
        if name not in self.dsem:
            self.dsem[name] = self.es.enter_context(
                self.nc.semaphore(f"dma_{name}{self.grp}"))
            self.dcnt[name] = 0
        return self.dsem[name]

    def op(self, eng, fn, waits=(), sig=False, dma=False, force=False,
           dma_sem=None):
        if dma:
            assert dma_sem is not None, "DMA ops need a named class semaphore"
            mysem = self._dma_sem(dma_sem)
        else:
            mysem = self.sem[eng]
        if eng == "v" and not dma:
            sig = True
            waits = list(waits) + [self.last_v]
            force = True
        wl = []
        for w in waits:
            if w is None:
                continue
            sem, val = w
            if sem is mysem and not force:
                continue
            key = id(sem)
            prev = self.waited[eng].get(key, -1)
            if val <= prev:
                continue
            self.waited[eng][key] = val
            wl.append((sem, val))
        tokn = None
        if sig:
            if dma:
                self.dcnt[dma_sem] += 16
                tokn = (mysem, self.dcnt[dma_sem])
            else:
                self.cnt[eng] += 1
                tokn = (mysem, self.cnt[eng])
        self.ops[eng].append((fn, wl, (mysem, 16 if dma else 1) if sig else None))
        if eng == "v" and not dma:
            self.last_v = tokn
        return tokn

    def flush(self):
        ops = self.ops
        self.ops = {e: [] for e in ENGS}

        def run(lst, engobj):
            for fn, wl, sig in lst:
                for sem, val in wl:
                    engobj.wait_ge(sem, val)
                instr = fn(engobj)
                if sig is not None:
                    sem, inc = sig
                    instr.then_inc(sem, inc)

        with self.nc.Block() as blk:
            if ops["t"]:
                @blk.tensor
                def _(e):
                    run(ops["t"], e)
            if ops["v"]:
                @blk.vector
                def _(e):
                    run(ops["v"], e)
            if ops["s"]:
                @blk.scalar
                def _(e):
                    run(ops["s"], e)
            if ops["y"]:
                @blk.sync
                def _(e):
                    run(ops["y"], e)
            if ops["g"]:
                @blk.gpsimd
                def _(e):
                    run(ops["g"], e)


def build(debug=False, max_heads=NH, halves=(0, 1), do_z=True,
          do_passb=True, do_ffn=True):
    nc = bass.Bass("TRN2", target_bir_lowering=False, debug=False)
    es = ExitStack()

    def din(name, shape, dt=F32):
        return nc.dram_tensor(name, list(shape), dt, kind="ExternalInput")

    dx = din("x", [T, D])
    dWin = din("W_in", [D, PROJ])
    dWout = din("W_out", [DSSM, D])
    dconv = din("conv_pk", [128, 264])
    ddtb = din("dt_bias_c", [64, 1])
    dAlog = din("A_log_c", [64, 1])
    dDrow = din("D_row", [1, 64])
    dnw = din("nw_rows", [64, 128])
    dAw = din("A_w", [512, D])
    dBw = din("B_w", [D, 512])
    dWq = din("Wq_w", [16, D])
    dkeys = din("keys", [64, 16])
    dIDbf = din("IDbf", [128, 128], BF16)
    dIDf = din("IDf32", [128, 128])
    dI64 = din("I64f", [64, 64])
    dI16 = din("I16bf", [16, 16], BF16)
    dSLT = din("SLT", [128, 128])
    dMASK = din("MASKNEG", [128, 128])
    dUT025 = din("UT025", [128, 128])
    dOR127 = din("ONEROW127", [128, 128])
    dSCM = din("SCANMASK", [64, TH])
    dEXP8 = din("EXP8", [64, 512], BF16)

    dout = nc.dram_tensor("out", [T, D], F32, kind="ExternalOutput")
    skind = "ExternalOutput" if debug else "Internal"
    dxn = nc.dram_tensor("xn_hbm", [T, D], BF16, kind=skind)
    dsz = nc.dram_tensor("sz_hbm", [T, DSSM], BF16, kind=skind)
    dygT = nc.dram_tensor("ygT_hbm", [NT * NH * 128, 128], BF16, kind=skind)
    dx1 = nc.dram_tensor("x1_hbm", [T, D], F32, kind=skind)
    dxn2 = nc.dram_tensor("xn2_hbm", [T, D], BF16, kind=skind)
    dWbf = nc.dram_tensor("wbf_hbm", [66 * 128, KD * 256], BF16, kind="Internal")

    def sb(stack, name, shape, dtype):
        return stack.enter_context(nc.sbuf_tensor(name, shape, dtype))

    cIDbf = sb(es, "cIDbf", [128, 128], BF16)
    cIDf = sb(es, "cIDf", [128, 128], F32)
    cI64 = sb(es, "cI64", [64, 64], F32)
    cI16 = sb(es, "cI16", [16, 16], BF16)
    cI64bf = sb(es, "cI64bf", [64, 64], BF16)
    cSLT = sb(es, "cSLT", [128, 128], F32)
    cMASK = sb(es, "cMASK", [128, 128], F32)
    cUT025 = sb(es, "cUT025", [128, 128], F32)
    cOR127 = sb(es, "cOR127", [128, 128], F32)
    cSCM = sb(es, "cSCM", [64, TH], F32)
    cconv = sb(es, "cconv", [128, 264], F32)
    cdtb = sb(es, "cdtb", [64, 1], F32)
    cAlog = sb(es, "cAlog", [64, 1], F32)
    cDrow = sb(es, "cDrow", [1, 64], F32)
    cnw = sb(es, "cnw", [64, 128], F32)
    cnwT = sb(es, "cnwT", [128, 64], F32)
    cDrep = sb(es, "cDrep", [128, 64], F32)
    zcol = sb(es, "zcol", [128, 1], F32)
    onecol = sb(es, "onecol", [128, 1], F32)
    epscol = sb(es, "epscol", [128, 1], F32)
    eps4col = sb(es, "eps4col", [128, 1], F32)
    ln05col = sb(es, "ln05col", [128, 1], F32)
    negEA = sb(es, "negEA", [64, 1], F32)
    ones1f = sb(es, "ones1f", [1, 128], F32)
    xtile = sb(es, "xtile", [128, D], F32)
    xnst = sb(es, "xnst", [128, D], BF16)
    s1c = sb(es, "s1c", [128, 1], F32)
    s2c = sb(es, "s2c", [128, 1], F32)
    t1c = sb(es, "t1c", [128, 1], F32)
    t2c = sb(es, "t2c", [128, 1], F32)
    rstdc = sb(es, "rstdc", [128, 1], F32)
    nmrc = sb(es, "nmrc", [128, 1], F32)
    stats = sb(es, "stats", [128, NT * 2], F32)
    rscale = sb(es, "rscale", [128, NT], F32)
    PS = [es.enter_context(nc.psum_tensor(f"ps{i}", [128, 512], F32))
          for i in range(8)]

    E = Em(nc, es)
    tok = {}
    bank_r = [None] * 8
    bank_w = [None] * 8

    LD = None
    for dram, tile in [(dIDbf, cIDbf), (dIDf, cIDf), (dI64, cI64),
                       (dI16, cI16), (dSLT, cSLT), (dMASK, cMASK),
                       (dUT025, cUT025), (dOR127, cOR127), (dSCM, cSCM),
                       (dconv, cconv), (ddtb, cdtb),
                       (dAlog, cAlog), (dDrow, cDrow), (dnw, cnw)]:
        LD = E.op("y", (lambda d_, t_: lambda e: e.dma_start(
            out=t_[:], in_=d_.ap()))(dram, tile), sig=True, dma=True, dma_sem="cst")

    E.op("v", lambda e: e.memset(zcol[:], 0.0))
    E.op("v", lambda e: e.memset(onecol[:], 1.0))
    E.op("v", lambda e: e.memset(epscol[:], EPS))
    E.op("v", lambda e: e.memset(eps4col[:], 4.0 * EPS))
    E.op("v", lambda e: e.memset(ln05col[:], float(np.log(0.5))))
    E.op("v", lambda e: e.memset(ones1f[:], 1.0))
    E.op("v", lambda e: e.memset(stats[:], 0.0))
    E.op("v", lambda e: e.tensor_copy(out=cI64bf[:], in_=cI64[:]), waits=[LD])
    VINIT = E.op("v", lambda e: e.memset(rscale[:], 0.0), sig=True)

    eA0 = E.op("s", lambda e: e.activation(out=negEA[:], in_=cAlog[:],
                                           func=AF.Exp, scale=1.0,
                                           bias=zcol[0:64, :]),
               waits=[LD, VINIT], sig=True)
    NEA = E.op("v", lambda e: e.tensor_scalar(out=negEA[:], in0=negEA[:],
                                              scalar1=-1.0, scalar2=None,
                                              op0=ALU.mult),
               waits=[eA0], sig=True)


    WRM = E.op("t", lambda e: e.matmul(out=PS[7][:, 0:128], lhsT=cSLT[:],
                                       rhs=cSLT[:], start=True, stop=True),
               waits=[LD, VINIT], sig=True)
    E.op("t", lambda e: e.matmul(out=PS[6][:, 0:64], lhsT=cnw[:], rhs=cI64[:],
                                 start=True, stop=True))
    T0 = E.op("t", lambda e: e.matmul(out=PS[6][:, 64:128], lhsT=ones1f[:],
                                      rhs=cDrow[:], start=True, stop=True),
              sig=True)
    E.op("v", lambda e: e.tensor_copy(out=cnwT[:], in_=PS[6][:, 0:64]),
         waits=[T0])
    CDR = E.op("v", lambda e: e.tensor_copy(out=cDrep[:], in_=PS[6][:, 64:128]),
               sig=True)
    bank_r[6] = CDR
    bank_r[7] = WRM

    ga = ExitStack()
    xnT = sb(ga, "xnT", [128, KD * TH], BF16)
    WCH = [sb(ga, f"WCH{i}", [128, KD * 256], BF16) for i in range(2)]
    Bbf = sb(ga, "Bbf", [128, TH], BF16)
    Cbf = sb(ga, "Cbf", [128, TH], BF16)
    BTt = sb(ga, "BTt", [128, NCH * 128], BF16)
    Gm = sb(ga, "Gm", [128, NCH * 128], BF16)
    cumT = sb(ga, "cumT", [128, NCH * 64], F32)
    dtTh = sb(ga, "dtTh", [128, NCH * 64], F32)
    dtbh = sb(ga, "dtbh", [128, NCH * 64], F32)
    DdtB = sb(ga, "DdtB", [128, NCH * 64], F32)
    AtotR = sb(ga, "AtotR", [128, NCH * 64], F32)
    cum_ht = sb(ga, "cum_ht", [64, TH], F32)
    dtF = sb(ga, "dtF", [64, TH], F32)
    logdA = sb(ga, "logdA", [64, TH], F32)
    sm64b = sb(ga, "sm64b", [128, 64], F32)
    sm64c = sb(ga, "sm64c", [128, 64], F32)
    bTbuf = sb(ga, "bTbuf", [128, 64], F32)
    tails = sb(ga, "tails", [128, 66 * 3], F32)
    raw_ext = sb(ga, "raw_ext", [128, TH + 4], BF16)
    acc_cv = sb(ga, "acc_cv", [128, TH], BF16)
    ubuf = sb(ga, "ubuf", [128, TH], BF16)
    xhbf = [sb(ga, f"xhbf{i}", [128, TH], BF16) for i in range(2)]
    szt = [sb(ga, f"szt{i}", [128, TH], BF16) for i in range(2)]
    Hbf = sb(ga, "Hbf", [128, NH * 128], BF16)
    dtxT = [[sb(ga, f"dtxT{i}_{j}", [128, 128], BF16) for j in range(2)]
            for i in range(2)]
    DtXb = [[sb(ga, f"DtXb{i}_{j}", [128, 128], BF16) for j in range(2)]
            for i in range(2)]
    Mske = [sb(ga, f"Mske{i}", [128, 128], F32) for i in range(2)]
    LTb = [sb(ga, f"LTb{i}", [128, 128], BF16) for i in range(2)]
    aBrd = [sb(ga, f"aBrd{i}", [128, 128], BF16) for i in range(2)]
    aCb = [[sb(ga, f"aCb{i}_{j}", [128, 128], BF16) for j in range(2)]
           for i in range(2)]
    STb = [[sb(ga, f"STb{i}_{j}", [128, 128], BF16) for j in range(2)]
           for i in range(2)]
    yDb = [sb(ga, f"yDb{i}", [128, 128], BF16) for i in range(2)]
    ygS = [sb(ga, f"ygS{i}", [128, 128], BF16) for i in range(2)]
    ygTS = [sb(ga, f"ygTS{i}", [128, 128], BF16) for i in range(2)]
    rmsb = sb(ga, "rmsb", [128, NT * 64], F32)
    dump = sb(ga, "dump", [128, 128], BF16)
    szstg = [sb(ga, f"szstg{i}", [128, 256], BF16) for i in range(2)]

    wmx = rmsb[:, 0:8]
    wmw = rmsb[:, 8:136]
    E.op("v", lambda e: e.max(out=wmx, in_=cSLT[:]), waits=[LD])
    E.op("v", lambda e: e.match_replace(out=wmw, in_to_replace=wmx,
                                        in_values=cSLT[:], imm_value=-1e30))
    E.op("v", lambda e: e.max(out=wmx, in_=wmw))
    E.op("v", lambda e: e.match_replace(out=wmw, in_to_replace=wmx,
                                        in_values=wmw, imm_value=-1e30))
    E.op("v", lambda e: e.reduce_sum(out=rmsb[:, 0:1], in_=wmw, axis=AX.X))
    E.op("v", lambda e: e.reciprocal(out=rmsb[:, 0:1], in_=onecol[:]))
    HINIT = E.op("v", lambda e: e.memset(Hbf[:], 0.0), sig=True)
    E.op("v", lambda e: e.memset(tails[:], 0.0))
    RMSI = E.op("v", lambda e: e.memset(rmsb[:], 0.0), sig=True)

    wch_tok = [None, None]
    wch_free = [None, None]
    wst_tok = [None, None]
    wslot = [0]

    def load_wchunk(c0, w, b, cached):
        slot = wslot[0]
        wslot[0] += 1
        war = wch_free[b]
        if cached:
            t_ = E.op("s", (lambda sl: lambda e: e.dma_start(
                out=WCH[b][:], in_=dWbf.ap()[sl * 128:(sl + 1) * 128, :]))(
                slot % 66),
                waits=[war, wst_tok[0], wst_tok[1], tok.get("wst_final")],
                sig=True, dma=True, dma_sem=f"wchc{b}")
            wch_tok[b] = t_
            return t_
        t_ = None
        for jj in range(4):
            src = dWin.ap()[jj * 8 * 128:(jj + 1) * 8 * 128, c0:c0 + w] \
                .rearrange("(k p) c -> p k c", p=128)
            dst = WCH[b][:, jj * 8 * 256:(jj + 1) * 8 * 256] \
                .rearrange("p (k c) -> p k c", k=8)[:, :, 0:w]
            t_ = E.op("g", (lambda s_, d_: lambda e: e.dma_start(
                out=d_, in_=s_))(src, dst), waits=[war, VINIT, wst_tok[b]],
                sig=True, dma=True, dma_sem=f"wch{b}")
        wch_tok[b] = t_
        wst_tok[b] = E.op("y", (lambda sl: lambda e: e.dma_start(
            out=dWbf.ap()[sl * 128:(sl + 1) * 128, :], in_=WCH[b][:]))(slot % 66),
            waits=[t_], sig=True, dma=True, dma_sem=f"wst{b}")
        tok["wst_final"] = wst_tok[b]
        return t_

    def phase_ln1(half):
        base = half * TH
        prev_norm = tok.get("ln_prev")
        last_store = tok.get("xns_last")
        for tb in range(NTH):
            lx = E.op("y", (lambda r0: lambda e: e.dma_start(
                out=xtile[:], in_=dx.ap()[r0:r0 + 128, :]))(base + tb * 128),
                waits=[prev_norm], sig=True, dma=True, dma_sem="xt")
            rs = E.op("v", lambda e: e.reduce_sum(out=s1c[:], in_=xtile[:],
                                                  axis=AX.X), waits=[lx],
                      sig=True)
            sq = E.op("s", lambda e: e.activation(out=xnst[:], in_=xtile[:],
                                                  func=AF.Square,
                                                  accum_out=s2c[:]),
                      waits=[lx], sig=True)
            E.op("v", lambda e: e.tensor_scalar(out=t1c[:], in0=s1c[:],
                                                scalar1=1.0 / D, scalar2=None,
                                                op0=ALU.mult),
                 waits=[sq, rs], force=True)
            E.op("v", lambda e: e.tensor_scalar(out=t2c[:], in0=s2c[:],
                                                scalar1=1.0 / D, scalar2=None,
                                                op0=ALU.mult))
            nv = E.op("v", lambda e: e.scalar_tensor_tensor(
                out=s1c[:], in0=t1c[:], scalar=t1c[:], in1=t2c[:],
                op0=ALU.mult, op1=ALU.subtract), sig=True)
            lg = E.op("s", lambda e: e.activation(out=t2c[:], in_=s1c[:],
                                                  func=AF.Ln, scale=-1.0,
                                                  bias=epscol[:]),
                      waits=[nv], sig=True)
            rst = E.op("s", lambda e: e.activation(out=rstdc[:], in_=t2c[:],
                                                   func=AF.Exp, scale=-0.5,
                                                   bias=zcol[:]),
                      waits=[lg], sig=True, force=True)
            E.op("v", lambda e: e.tensor_scalar(
                out=t2c[:], in0=t1c[:], scalar1=rstdc[:], scalar2=None,
                op0=ALU.mult), waits=[rst])
            nm = E.op("v", lambda e: e.tensor_scalar(
                out=nmrc[:], in0=t2c[:], scalar1=-1.0, scalar2=None,
                op0=ALU.mult), sig=True)
            norm = E.op("s", lambda e: e.activation(out=xnst[:], in_=xtile[:],
                                                    func=AF.Identity,
                                                    scale=rstdc[:],
                                                    bias=nmrc[:]),
                        waits=[nm, rst, last_store], sig=True, force=True)
            prev_norm = norm
            last_store = E.op("y", (lambda r0: lambda e: e.dma_start(
                out=dxn.ap()[r0:r0 + 128, :], in_=xnst[:]))(base + tb * 128),
                waits=[norm], sig=True, dma=True, dma_sem="xns")
        tok["ln_prev"] = prev_norm
        tok["xns_last"] = last_store

        tr = None
        for k in range(KD):
            tr = E.op("s", (lambda k_: lambda e: e.dma_start(
                out=xnT[:, k_ * TH:(k_ + 1) * TH],
                in_=dxn.ap()[base:base + TH, k_ * 128:(k_ + 1) * 128],
                transpose=True))(k),
                waits=[last_store, tok.get("xnT_free")], sig=True, dma=True,
                dma_sem="xnt")
        tok["xnT_ready"] = tr

    def phase_bcd(half):
        cached = half != halves[0]
        lbc = load_wchunk(16384, 256, 0, cached)
        ldt = load_wchunk(16640, 64, 1, cached)
        XR = tok["xnT_ready"]

        raw_free = tok.get("raw_free")
        for cb, dstbuf, blkid in [(0, Bbf, 64), (1, Cbf, 65)]:
            evs = []
            for tg in range(2):
                pb = tg
                mm = None
                for k in range(KD):
                    mm = E.op("t", (lambda k_, tg_, cb_, s_, p_: lambda e:
                                    e.matmul(
                        out=PS[tg_][:, 0:512],
                        lhsT=WCH[0][:, k_ * 256 + cb_ * 128:
                                    k_ * 256 + cb_ * 128 + 128],
                        rhs=xnT[:, k_ * TH + tg_ * 512:
                                k_ * TH + tg_ * 512 + 512],
                        start=s_, stop=p_))(k, tg, cb, k == 0, k == KD - 1),
                        waits=[XR, lbc, bank_r[pb]] if k == 0 else (),
                        sig=(k == KD - 1))
                bank_w[pb] = mm
                ev = E.op("s", (lambda tg_: lambda e: e.copy(
                    out=raw_ext[:, 4 + tg_ * 512: 4 + tg_ * 512 + 512],
                    in_=PS[tg_][:, 0:512]))(tg),
                    waits=[mm, raw_free], sig=True)
                evs.append(ev)
                bank_r[pb] = ev
            if cb == 1:
                wch_free[0] = bank_w[1]
            E.op("v", (lambda b_: lambda e: e.tensor_copy(
                out=raw_ext[:, 1:4], in_=tails[:, b_ * 3: b_ * 3 + 3]))(blkid),
                waits=evs)
            E.op("v", (lambda b_: lambda e: e.tensor_scalar(
                out=acc_cv[:], in0=raw_ext[:, 4:4 + TH],
                scalar1=cconv[:, b_ * 4 + 3: b_ * 4 + 4], scalar2=None,
                op0=ALU.mult))(blkid))
            for k in range(3):
                E.op("v", (lambda b_, k_: lambda e: e.scalar_tensor_tensor(
                    out=acc_cv[:], in0=raw_ext[:, k_ + 1: k_ + 1 + TH],
                    scalar=cconv[:, b_ * 4 + k_: b_ * 4 + k_ + 1],
                    in1=acc_cv[:], op0=ALU.mult, op1=ALU.add))(blkid, k))
            cv = E.op("v", (lambda b_: lambda e: e.tensor_copy(
                out=tails[:, b_ * 3: b_ * 3 + 3],
                in_=raw_ext[:, 1 + TH: 4 + TH]))(blkid), sig=True)
            raw_free = cv
            th_ = E.op("s", lambda e: e.activation(out=ubuf[:], in_=acc_cv[:],
                                                   func=AF.Tanh, scale=0.5,
                                                   bias=zcol[:]),
                       waits=[cv], sig=True)
            sil = E.op("v", (lambda d_: lambda e: e.scalar_tensor_tensor(
                out=d_[:], in0=ubuf[:], scalar=1.0, in1=acc_cv[:],
                op0=ALU.add, op1=ALU.mult))(dstbuf), waits=[th_], sig=True)
            tok[f"bc{cb}"] = sil
        tok["raw_free"] = raw_free

        dtevs = []
        for tg in range(2):
            pb = tg
            mm = None
            for k in range(KD):
                mm = E.op("t", (lambda k_, tg_, s_, p_: lambda e: e.matmul(
                    out=PS[tg_][0:64, 0:512],
                    lhsT=WCH[1][:, k_ * 256: k_ * 256 + 64],
                    rhs=xnT[:, k_ * TH + tg_ * 512: k_ * TH + tg_ * 512 + 512],
                    start=s_, stop=p_))(k, tg, k == 0, k == KD - 1),
                    waits=[XR, ldt, bank_r[pb]] if k == 0 else (),
                    sig=(k == KD - 1))
            bank_w[pb] = mm
            ev = E.op("s", (lambda tg_: lambda e: e.activation(
                out=dtF[:, tg_ * 512: tg_ * 512 + 512], in_=PS[tg_][0:64, 0:512],
                func=AF.Identity, scale=1.0, bias=cdtb[:]))(tg),
                waits=[mm], sig=True)
            dtevs.append(ev)
            bank_r[pb] = ev
        wch_free[1] = bank_w[1]

        ab = E.op("s", lambda e: e.activation(
            out=logdA[:], in_=dtF[:], func=AF.Abs, scale=1.0,
            bias=zcol[0:64, :]), waits=dtevs + [NEA, tok.get("scan_done")],
            sig=True)
        E.op("s", lambda e: e.activation(out=cum_ht[:], in_=logdA[:],
                                         func=AF.Exp, scale=-1.0,
                                         bias=zcol[0:64, :]), waits=[ab])
        ln_ = E.op("s", lambda e: e.activation(out=logdA[:], in_=cum_ht[:],
                                               func=AF.Ln, scale=1.0,
                                               bias=onecol[0:64, :]), sig=True)
        E.op("v", lambda e: e.tensor_scalar(out=cum_ht[:], in0=dtF[:],
                                            scalar1=0.0, scalar2=None,
                                            op0=ALU.max), waits=[ln_])
        E.op("v", lambda e: e.tensor_add(out=dtF[:], in0=cum_ht[:], in1=logdA[:]))
        E.op("v", lambda e: e.tensor_scalar(out=logdA[:], in0=dtF[:],
                                            scalar1=negEA[:], scalar2=None,
                                            op0=ALU.mult))
        CUM = E.op("v", lambda e: e.tensor_tensor_scan(
            out=cum_ht[:], data0=cSCM[:], data1=logdA[:], initial=0.0,
            op0=ALU.mult, op1=ALU.add), sig=True)

        gm = None
        sg = tok.get("scan_guard", [])
        for c in range(NCH):
            E.op("t", (lambda c_: lambda e: e.matmul(
                out=PS[2][:, 0:64], lhsT=cum_ht[:, c_ * Q:(c_ + 1) * Q],
                rhs=cI64[:], start=True, stop=True))(c),
                waits=[CUM, bank_r[2]] + (sg if c == 0 else []))
            E.op("t", (lambda c_: lambda e: e.matmul(
                out=PS[2][:, 64:128], lhsT=dtF[:, c_ * Q:(c_ + 1) * Q],
                rhs=cI64[:], start=True, stop=True))(c))
            m3 = E.op("t", (lambda c_: lambda e: e.matmul(
                out=PS[2][:, 128:192], lhsT=logdA[:, c_ * Q:(c_ + 1) * Q],
                rhs=cI64[:], start=True, stop=True))(c), sig=True)
            bank_w[2] = m3
            e1 = E.op("v", (lambda c_: lambda e: e.tensor_copy(
                out=cumT[:, c_ * 64:(c_ + 1) * 64], in_=PS[2][:, 0:64]))(c),
                waits=[m3] + (sg if c == 0 else []), sig=True)
            E.op("v", (lambda c_: lambda e: e.tensor_scalar(
                out=dtTh[:, c_ * 64:(c_ + 1) * 64], in0=PS[2][:, 64:128],
                scalar1=0.5, scalar2=None, op0=ALU.mult))(c))
            rc = E.op("v", lambda e: e.reciprocal(out=sm64c[:],
                                                  in_=PS[2][:, 64:128]),
                      sig=True)
            E.op("v", (lambda c_: lambda e: e.tensor_mul(
                out=DdtB[:, c_ * 64:(c_ + 1) * 64], in0=sm64c[:],
                in1=cDrep[:]))(c), waits=[CDR, rc], force=True)
            e2 = E.op("v", lambda e: e.tensor_copy(out=sm64b[:],
                                                   in_=PS[2][:, 128:192]),
                      sig=True)
            bank_r[2] = e2
            E.op("t", lambda e: e.matmul(out=PS[3][:, 0:64], lhsT=cSLT[:],
                                         rhs=sm64b[:], start=True, stop=True),
                 waits=[e2, bank_r[3]])
            s2_ = E.op("t", (lambda c_: lambda e: e.matmul(
                out=PS[3][:, 64:128], lhsT=cOR127[:],
                rhs=cumT[:, c_ * 64:(c_ + 1) * 64], start=True, stop=True))(c),
                waits=[e1] + (sg if c == 0 else []), sig=True)
            bank_w[3] = s2_
            bt = E.op("s", lambda e: e.activation(out=bTbuf[:],
                                                  in_=PS[3][:, 0:64],
                                                  func=AF.Exp, scale=1.0,
                                                  bias=zcol[:]),
                      waits=[s2_], sig=True)
            at = E.op("s", (lambda c_: lambda e: e.activation(
                out=AtotR[:, c_ * 64:(c_ + 1) * 64], in_=PS[3][:, 64:128],
                func=AF.Exp, scale=1.0, bias=zcol[:]))(c), sig=True)
            bank_r[3] = at
            E.op("v", (lambda c_: lambda e: e.tensor_mul(
                out=dtbh[:, c_ * 64:(c_ + 1) * 64],
                in0=dtTh[:, c_ * 64:(c_ + 1) * 64], in1=bTbuf[:]))(c),
                waits=[bt])
            b1 = E.op("t", (lambda c_: lambda e: e.matmul(
                out=PS[5][:, 0:128], lhsT=Bbf[:, c_ * Q:(c_ + 1) * Q],
                rhs=cIDbf[:], start=True, stop=True))(c),
                waits=[tok["bc0"], bank_r[5]] + (sg if c == 0 else []),
                sig=True)
            bank_w[5] = b1
            g1 = E.op("t", (lambda c_: lambda e: e.matmul(
                out=PS[4][:, 0:128], lhsT=Bbf[:, c_ * Q:(c_ + 1) * Q],
                rhs=Cbf[:, c_ * Q:(c_ + 1) * Q], start=True, stop=True))(c),
                waits=[tok["bc1"], bank_r[4]] + (sg if c == 0 else []),
                sig=True)
            bank_w[4] = g1
            bt2 = E.op("s", (lambda c_: lambda e: e.activation(
                out=BTt[:, c_ * 128:(c_ + 1) * 128], in_=PS[5][:, 0:128],
                func=AF.Identity, scale=0.5, bias=zcol[:]))(c),
                waits=[b1], sig=True)
            bank_r[5] = bt2
            gm = E.op("v", (lambda c_: lambda e: e.tensor_mul(
                out=Gm[:, c_ * 128:(c_ + 1) * 128], in0=PS[4][:, 0:128],
                in1=cUT025[:]))(c), waits=[g1], sig=True)
            bank_r[4] = gm
        tok["bcd_done"] = gm

    def phase_z(half):
        base = half * TH
        XR = tok["xnT_ready"]
        szfree = [None, None]
        ubfree = [None, None]
        g = -1
        cached = half != halves[0]
        for zc in range(32):
            lt = load_wchunk(256 * zc, 256, zc % 2, cached)
            for tb in range(NTH):
                g += 1
                pb = g % 2
                sg = g % 2
                mm = None
                for k in range(KD):
                    mm = E.op("t", (lambda b_, k_, tb_, pb_, s_, p_:
                                    lambda e: e.matmul(
                        out=PS[pb_][:, 0:256],
                        lhsT=xnT[:, k_ * TH + tb_ * 128:
                                 k_ * TH + tb_ * 128 + 128],
                        rhs=WCH[b_][:, k_ * 256:(k_ + 1) * 256],
                        start=s_, stop=p_))(zc % 2, k, tb, pb,
                                            k == 0, k == KD - 1),
                        waits=[XR, lt, bank_r[pb]] if k == 0 else (),
                        sig=(k == KD - 1))
                bank_w[pb] = mm
                if tb == NTH - 1:
                    wch_free[zc % 2] = mm
                th_ = E.op("s", (lambda pb_, sg_: lambda e: e.activation(
                    out=ubuf[:, sg_ * 256: sg_ * 256 + 256],
                    in_=PS[pb_][:, 0:256], func=AF.Tanh, scale=0.5,
                    bias=zcol[:]))(pb, sg),
                    waits=[mm, ubfree[sg]], sig=True)
                st = E.op("v", (lambda pb_, sg_: lambda e:
                                e.scalar_tensor_tensor(
                    out=szstg[sg_][:], in0=ubuf[:, sg_ * 256: sg_ * 256 + 256],
                    scalar=1.0, in1=PS[pb_][:, 0:256], op0=ALU.add,
                    op1=ALU.mult))(pb, sg),
                    waits=[th_, szfree[sg]], sig=True)
                bank_r[pb] = st
                ubfree[sg] = st
                szfree[sg] = E.op("y", (lambda r0, c0, sg_: lambda e:
                                        e.dma_start(
                    out=dsz.ap()[r0:r0 + 128, c0:c0 + 256],
                    in_=szstg[sg_][:]))(base + tb * 128, zc * 256, sg),
                    waits=[st], sig=True, dma=True, dma_sem=f"szst{sg}")
        tok["z_done"] = list(szfree)

    def phase_scan(half):
        XR = tok["xnT_ready"]
        cached = half != halves[0]
        pv = tok.setdefault("scan_pv", [{}, {}])
        last_proj = None
        assert max_heads % 2 == 0
        for p in range(max_heads // 2):
            wb = p % 2
            lt = load_wchunk(DSSM + 256 * p, 256, wb, cached)
            sz_ld = [None, None]
            sil = [None, None]
            for par in (0, 1):
                h = 2 * p + par
                srcv = dsz.ap()[half * TH:(half + 1) * TH,
                                h * 128:(h + 1) * 128].rearrange(
                    "(c p) w -> p c w", p=128)
                dstv = szt[par][:].rearrange("p (c w) -> p c w", c=NCH)
                zdw = tok.get("z_done") if do_z else [None, None]
                sz_ld[par] = E.op("y", (lambda s_, d_: lambda e: e.dma_start(
                    out=d_, in_=s_))(srcv, dstv),
                    waits=[pv[par].get("yg"), zdw[0], zdw[1]],
                    sig=True, dma=True, dma_sem=f"szld{par}")
                ptoks = []
                for tg in range(2):
                    mm = None
                    for k in range(KD):
                        mm = E.op("t", (lambda k_, sl, tg_, s_, p_, wb_:
                                        lambda e: e.matmul(
                            out=PS[tg_][:, 0:512],
                            lhsT=WCH[wb_][:, k_ * 256 + sl * 128:
                                          k_ * 256 + sl * 128 + 128],
                            rhs=xnT[:, k_ * TH + tg_ * 512:
                                    k_ * TH + tg_ * 512 + 512],
                            start=s_, stop=p_))(k, par, tg, k == 0,
                                                k == KD - 1, wb),
                            waits=[XR, lt, bank_r[tg]] if k == 0 else (),
                            sig=(k == KD - 1))
                    ptoks.append(mm)
                    bank_w[tg] = mm
                last_proj = ptoks[-1]
                if par == 1:
                    wch_free[wb] = ptoks[-1]
                evs = []
                for tg in range(2):
                    ev = E.op("s", (lambda tg_: lambda e: e.copy(
                        out=raw_ext[:, 4 + tg_ * 512: 4 + tg_ * 512 + 512],
                        in_=PS[tg_][:, 0:512]))(tg),
                        waits=[ptoks[tg], tok.get("raw_free")], sig=True)
                    evs.append(ev)
                    bank_r[tg] = ev
                E.op("v", (lambda b_: lambda e: e.tensor_copy(
                    out=raw_ext[:, 1:4],
                    in_=tails[:, b_ * 3: b_ * 3 + 3]))(h), waits=evs)
                E.op("v", (lambda b_: lambda e: e.tensor_scalar(
                    out=acc_cv[:], in0=raw_ext[:, 4:4 + TH],
                    scalar1=cconv[:, b_ * 4 + 3: b_ * 4 + 4], scalar2=None,
                    op0=ALU.mult))(h))
                for k in range(3):
                    E.op("v", (lambda b_, k_: lambda e: e.scalar_tensor_tensor(
                        out=acc_cv[:], in0=raw_ext[:, k_ + 1: k_ + 1 + TH],
                        scalar=cconv[:, b_ * 4 + k_: b_ * 4 + k_ + 1],
                        in1=acc_cv[:], op0=ALU.mult, op1=ALU.add))(h, k))
                cv = E.op("v", (lambda b_: lambda e: e.tensor_copy(
                    out=tails[:, b_ * 3: b_ * 3 + 3],
                    in_=raw_ext[:, 1 + TH: 4 + TH]))(h), sig=True)
                tok["raw_free"] = cv
                th_ = E.op("s", lambda e: e.activation(
                    out=ubuf[:], in_=acc_cv[:], func=AF.Tanh, scale=0.5,
                    bias=zcol[:]), waits=[cv], sig=True)
                sil[par] = E.op("v", (lambda par_: lambda e:
                                      e.scalar_tensor_tensor(
                    out=xhbf[par_][:], in0=ubuf[:], scalar=1.0, in1=acc_cv[:],
                    op0=ALU.add, op1=ALU.mult))(par),
                    waits=[th_, pv[par].get("x1")], sig=True)

            def emit_SX(c):
                for par in (0, 1):
                    h = 2 * p + par
                    cc = c * 64 + h
                    sl_ = c % 2
                    pb2 = 2 + par
                    x1_ = E.op("t", (lambda c_, par_: lambda e: e.matmul(
                        out=PS[2 + par_][:, 0:128],
                        lhsT=xhbf[par_][:, c_ * Q:(c_ + 1) * Q],
                        rhs=cIDbf[:], start=True, stop=True))(c, par),
                        waits=[sil[par], pv[par].get("dt1p"),
                               pv[par].get("dt2p"), pv[par].get("mk"),
                               pv[par].get("abp")], sig=True)
                    pv[par]["x1"] = x1_
                for par in (0, 1):
                    h = 2 * p + par
                    cc = c * 64 + h
                    cb_ = E.op("s", (lambda cc_, par_: lambda e: e.activation(
                        out=Mske[par_][:], in_=cIDf[:], func=AF.Identity,
                        scale=0.0, bias=cumT[:, cc_:cc_ + 1]))(cc, par),
                        waits=[tok["bcd_done"], pv[par].get("mk")], sig=True)
                    pv[par]["cb"] = cb_
                for par in (0, 1):
                    sl_ = c % 2
                    ct_ = E.op("t", (lambda par_: lambda e: e.matmul(
                        out=PS[2 + par_][:, 128:256],
                        lhsT=Mske[par_][:], rhs=cIDf[:],
                        start=True, stop=True))(par),
                        waits=[pv[par]["cb"]], sig=True)
                    pv[par]["ct"] = ct_
                for par in (0, 1):
                    h = 2 * p + par
                    cc = c * 64 + h
                    sl_ = c % 2
                    mk = E.op("v", (lambda cc_, par_: lambda e:
                                    e.scalar_tensor_tensor(
                        out=Mske[par_][:],
                        in0=PS[2 + par_][:, 128:256],
                        scalar=cumT[:, cc_:cc_ + 1], in1=cMASK[:],
                        op0=ALU.subtract, op1=ALU.add))(cc, par),
                        waits=[pv[par]["ct"]], sig=True)
                    pv[par]["mk"] = mk
                for par in (0, 1):
                    sl_ = c % 2
                    lt_ = E.op("s", (lambda par_: lambda e: e.activation(
                        out=LTb[par_][:], in_=Mske[par_][:], func=AF.Exp,
                        scale=1.0, bias=zcol[:]))(par),
                        waits=[pv[par]["mk"]], sig=True)
                    ab_ = E.op("s", (lambda par_: lambda e: e.activation(
                        out=aBrd[par_][:],
                        in_=PS[2 + par_][:, 128:256],
                        func=AF.Exp, scale=1.0, bias=ln05col[:]))(par),
                        sig=True)
                    pv[par]["lt"] = lt_
                    pv[par]["abp"] = ab_
                for par in (0, 1):
                    sl_ = c % 2
                    st_ = E.op("v", (lambda c_, par_, s_: lambda e:
                                     e.tensor_mul(
                        out=STb[par_][s_][:], in0=LTb[par_][:],
                        in1=Gm[:, c_ * 128:(c_ + 1) * 128]))(c, par, sl_),
                        waits=[pv[par]["lt"], pv[par].get(f"y2_{sl_}")],
                        sig=True)
                    ac_ = E.op("v", (lambda c_, par_, s_: lambda e:
                                     e.tensor_mul(
                        out=aCb[par_][s_][:], in0=aBrd[par_][:],
                        in1=Cbf[:, c_ * Q:(c_ + 1) * Q]))(c, par, sl_),
                        waits=[pv[par]["abp"]], sig=True)
                    pv[par][f"st_{sl_}"] = st_
                    pv[par][f"acb_{sl_}"] = ac_
                for par in (0, 1):
                    h = 2 * p + par
                    cc = c * 64 + h
                    sl_ = c % 2
                    dt1 = E.op("v", (lambda cc_, par_, s_: lambda e:
                                     e.tensor_scalar(
                        out=dtxT[par_][s_][:],
                        in0=PS[2 + par_][:, 0:128],
                        scalar1=dtTh[:, cc_:cc_ + 1], scalar2=None,
                        op0=ALU.mult))(cc, par, sl_),
                        waits=[pv[par]["x1"],
                               pv[par].get(f"y2_{sl_}"),
                               pv[par].get(f"yd_{sl_}")], sig=True)
                    dt2 = E.op("v", (lambda cc_, par_, s_: lambda e:
                                     e.tensor_scalar(
                        out=DtXb[par_][s_][:],
                        in0=PS[2 + par_][:, 0:128],
                        scalar1=dtbh[:, cc_:cc_ + 1], scalar2=None,
                        op0=ALU.mult))(cc, par, sl_),
                        waits=[pv[par].get(f"d1_{sl_}")], sig=True)
                    pv[par]["dt1p"] = dt1
                    pv[par]["dt2p"] = dt2
                    pv[par][f"dt1_{sl_}"] = dt1
                    pv[par][f"dt2_{sl_}"] = dt2

            def emit_Y(c):
                for par in (0, 1):
                    h = 2 * p + par
                    sl_ = c % 2
                    E.op("t", (lambda h_, par_, s_: lambda e: e.matmul(
                        out=PS[4 + par_][:, 0:128], lhsT=aCb[par_][s_][:],
                        rhs=Hbf[:, h_ * 128:(h_ + 1) * 128], start=True,
                        stop=False))(h, par, sl_),
                        waits=[pv[par][f"acb_{sl_}"], pv[par].get("hu"),
                               HINIT, pv[par].get("ye")])
                    y2 = E.op("t", (lambda par_, s_: lambda e: e.matmul(
                        out=PS[4 + par_][:, 0:128], lhsT=STb[par_][s_][:],
                        rhs=dtxT[par_][s_][:], start=False, stop=True))(par, sl_),
                        waits=[pv[par][f"st_{sl_}"], pv[par][f"dt1_{sl_}"]],
                        sig=True)
                    pv[par][f"y2_{sl_}"] = y2
                    pv[par]["y2"] = y2
                    d1 = E.op("t", (lambda c_, par_, s_: lambda e: e.matmul(
                        out=PS[6 + par_][:, 0:128],
                        lhsT=BTt[:, c_ * 128:(c_ + 1) * 128],
                        rhs=DtXb[par_][s_][:], start=True, stop=True))(c, par, sl_),
                        waits=[pv[par][f"dt2_{sl_}"], pv[par].get("hu")],
                        sig=True)
                    pv[par][f"d1_{sl_}"] = d1
                    pv[par]["d1"] = d1
                for par in (0, 1):
                    h = 2 * p + par
                    cc = c * 64 + h
                    hu = E.op("v", (lambda h_, cc_, par_: lambda e:
                                    e.scalar_tensor_tensor(
                        out=Hbf[:, h_ * 128:(h_ + 1) * 128],
                        in0=Hbf[:, h_ * 128:(h_ + 1) * 128],
                        scalar=AtotR[:, cc_:cc_ + 1],
                        in1=PS[6 + par_][:, 0:128],
                        op0=ALU.mult, op1=ALU.add))(h, cc, par),
                        waits=[pv[par]["d1"]], sig=True)
                    pv[par]["hu"] = hu
                for par in (0, 1):
                    h = 2 * p + par
                    cc = c * 64 + h
                    sl_ = c % 2
                    yd = E.op("v", (lambda cc_, par_, s_: lambda e:
                                    e.scalar_tensor_tensor(
                        out=yDb[par_][:], in0=dtxT[par_][s_][:],
                        scalar=DdtB[:, cc_:cc_ + 1],
                        in1=PS[4 + par_][:, 0:128],
                        op0=ALU.mult, op1=ALU.add))(cc, par, sl_),
                        waits=[pv[par][f"y2_{sl_}"]], sig=True)
                    pv[par][f"yd_{sl_}"] = yd
                    yg = E.op("v", (lambda c_, par_: lambda e: e.tensor_mul(
                        out=ygS[par_][:], in0=yDb[par_][:],
                        in1=szt[par_][:, c_ * 128:(c_ + 1) * 128]))(c, par),
                        waits=[sz_ld[par], yd, pv[par].get("yt")], sig=True)
                    pv[par]["yg"] = yg
                for par in (0, 1):
                    h = 2 * p + par
                    E.op("s", (lambda par_, col_: lambda e: e.activation(
                        out=dump[:, 0:128], in_=ygS[par_][:],
                        func=AF.Square,
                        accum_out=rmsb[:, col_:col_ + 1]))(
                        par, (half * NCH + c) * 64 + h),
                        waits=[pv[par]["yg"], RMSI])
                for par in (0, 1):
                    yt = E.op("t", (lambda par_: lambda e: e.matmul(
                        out=PS[4 + par_][:, 128:256], lhsT=ygS[par_][:],
                        rhs=cIDbf[:], start=True, stop=True))(par),
                        waits=[pv[par]["yg"]], sig=True)
                    pv[par]["yt"] = yt
                for par in (0, 1):
                    h = 2 * p + par
                    ye = E.op("s", (lambda par_, h_: lambda e: e.activation(
                        out=ygTS[par_][:], in_=PS[4 + par_][:, 128:256],
                        func=AF.Identity, scale=cnwT[:, h_:h_ + 1],
                        bias=zcol[:]))(par, h),
                        waits=[pv[par]["yt"], pv[par].get("ygst")], sig=True)
                    pv[par]["ye"] = ye
                    r0 = ((half * NCH + c) * 64 + h) * 128
                    stt = E.op("y", (lambda par_, r0_: lambda e: e.dma_start(
                        out=dygT.ap()[r0_:r0_ + 128, :],
                        in_=ygTS[par_][:]))(par, r0),
                        waits=[ye], sig=True, dma=True,
                        dma_sem=f"ygst{par}")
                    pv[par]["ygst"] = stt

            emit_SX(0)
            for c in range(NCH):
                if c + 1 < NCH:
                    emit_SX(c + 1)
                emit_Y(c)
        tok["xnT_free"] = last_proj
        tok["ygts_last"] = [pv[0].get("ygst"), pv[1].get("ygst")]
        tok["scan_guard"] = [pv[q].get(k) for q in (0, 1)
                             for k in ("dt1p", "dt2p", "abp", "mk",
                                       "yd_0", "yd_1", "ye", "hu")]

    for half in halves:
        phase_ln1(half)
        phase_bcd(half)
        if do_z:
            phase_z(half)
        phase_scan(half)
    if debug:
        dbg_bufs = [("s1c", s1c), ("s2c", s2c), ("t1c", t1c), ("t2c", t2c),
                    ("rstdc", rstdc), ("nmrc", nmrc),
                    ("Bbf", Bbf), ("Cbf", Cbf), ("BTt", BTt), ("Gm", Gm),
                    ("cumT", cumT), ("dtTh", dtTh), ("dtbh", dtbh),
                    ("DdtB", DdtB), ("AtotR", AtotR), ("dtF", dtF),
                    ("logdA", logdA), ("cum_ht", cum_ht),
                    ("xhbf0", xhbf[0]), ("xhbf1", xhbf[1]),
                    ("acc_cv", acc_cv), ("Hbf", Hbf)]
        dwaits = [tok["ygts_last"][0], tok["ygts_last"][1],
                  tok["scan_pv"][0].get("hu"), tok["scan_pv"][1].get("hu"),
                  tok.get("bcd_done")]
        for nm_, buf in dbg_bufs:
            dt_ = nc.dram_tensor("dbg_" + nm_, list(buf[:].shape),
                                 buf[:].dtype, kind="ExternalOutput")
            E.op("y", (lambda b_, d_: lambda e: e.dma_start(
                out=d_.ap(), in_=b_[:]))(buf, dt_), waits=dwaits,
                sig=True, dma=True, dma_sem="dbg")
    E.flush()

    scan_tail = [tok["ygts_last"][0], tok["ygts_last"][1],
                 tok["scan_pv"][0].get("hu"), tok["scan_pv"][1].get("hu")]

    if not do_passb:
        E.op("y", lambda e: e.nop(),
             waits=scan_tail[:2] + [tok.get("ln_prev")], force=True)
        E.flush()
        return nc

    E.new_group()
    RSC = None
    for tb in range(NT):
        rr = E.op("v", (lambda tb_: lambda e: e.reduce_sum(
            out=t1c[:], in_=rmsb[:, tb_ * 64:(tb_ + 1) * 64], axis=AX.X))(tb),
            waits=scan_tail + [RSC], sig=True)
        lg2 = E.op("s", lambda e: e.activation(out=t2c[:], in_=t1c[:],
                                               func=AF.Ln, scale=1.0 / DSSM,
                                               bias=eps4col[:]),
                   waits=[rr], sig=True)
        RSC = E.op("s", (lambda tb_: lambda e: e.activation(
            out=rscale[:, tb_:tb_ + 1], in_=t2c[:], func=AF.Exp, scale=-0.5,
            bias=zcol[:]))(tb), waits=[lg2], sig=True, force=True)
    E.flush()
    ga.close()

    gb = ExitStack()
    Wpan = sb(gb, "Wpan", [128, 64 * 1024], BF16)
    ygp = [sb(gb, f"ygp{i}", [128, 64 * 128], BF16) for i in range(2)]
    xres = [sb(gb, f"xres{i}", [128, 1024], F32) for i in range(2)]
    x1stg = [sb(gb, f"x1stg{i}", [128, 512], F32) for i in range(2)]
    dumpB = sb(gb, "dumpB", [128, 512], BF16)

    x1_store = {}

    def phase_passb():
        GATE = [RSC, scan_tail[0], scan_tail[1]]
        panel_cons = {}
        for g in range(4):
            subl = {}
            for j in range(8):
                src = dWout.ap()[j * 1024:(j + 1) * 1024,
                                 g * 1024:(g + 1) * 1024].rearrange(
                    "(k p) c -> p k c", p=128)
                dstv = Wpan[:, j * 8 * 1024:(j + 1) * 8 * 1024].rearrange(
                    "p (k c) -> p k c", k=8)
                subl[j] = E.op("g", (lambda s_, d_: lambda e: e.dma_start(
                    out=d_, in_=s_))(src, dstv),
                    waits=GATE + [panel_cons.get(g - 1)], sig=True, dma=True,
                    dma_sem="wout")
            for tb in range(NT):
                ypl = None
                for jj in range(8):
                    src = dygT.ap()[(tb * 64 + jj * 8) * 128:
                                    (tb * 64 + jj * 8 + 8) * 128, :].rearrange(
                        "(k p) c -> p k c", p=128)
                    dstv = ygp[tb % 2][:, jj * 8 * 128:(jj + 1) * 8 * 128] \
                        .rearrange("p (k c) -> p k c", k=8)
                    ypl = E.op("y", (lambda s_, d_: lambda e: e.dma_start(
                        out=d_, in_=s_))(src, dstv),
                        waits=[scan_tail[0], scan_tail[1],
                               tok.get(f"ygp_free{tb % 2}")],
                        sig=True, dma=True, dma_sem=f"ygp{tb % 2}")
                xr = E.op("y", (lambda tb_, g_, i_: lambda e: e.dma_start(
                    out=xres[i_][:],
                    in_=dx.ap()[tb_ * 128:(tb_ + 1) * 128,
                                g_ * 1024:(g_ + 1) * 1024]))(tb, g, tb % 2),
                    waits=[tok.get(f"xres_free{tb % 2}")], sig=True, dma=True,
                    dma_sem=f"xres{tb % 2}")
                for ds in range(2):
                    pb = (tb * 2 + ds) % 4
                    mm = None
                    for k in range(64):
                        if k == 0:
                            w = [ypl, bank_r[pb], subl[7]]
                        else:
                            w = ()
                        mm = E.op("t", (lambda tb_, k_, ds_, pb_, s_, p_:
                                        lambda e: e.matmul(
                            out=PS[pb_][:, 0:512],
                            lhsT=ygp[tb_ % 2][:, k_ * 128:(k_ + 1) * 128],
                            rhs=Wpan[:, k_ * 1024 + ds_ * 512:
                                     k_ * 1024 + ds_ * 512 + 512],
                            start=s_, stop=p_))(tb, k, ds, pb, k == 0, k == 63),
                            waits=w, sig=(k == 63))
                    bank_w[pb] = mm
                    if ds == 1:
                        tok[f"ygp_free{tb % 2}"] = mm
                        panel_cons[g] = mm
                    ev = E.op("v", (lambda tb_, ds_, pb_, i_: lambda e:
                                    e.scalar_tensor_tensor(
                        out=x1stg[i_][:], in0=PS[pb_][:, 0:512],
                        scalar=rscale[:, tb_:tb_ + 1],
                        in1=xres[tb_ % 2][:, ds_ * 512: ds_ * 512 + 512],
                        op0=ALU.mult, op1=ALU.add))(tb, ds, pb, ds),
                        waits=[mm, xr, RSC, tok.get(f"x1s_free{ds}")],
                        sig=True)
                    bank_r[pb] = ev
                    if ds == 1:
                        tok[f"xres_free{tb % 2}"] = ev
                    r1 = E.op("v", (lambda i_: lambda e: e.reduce_sum(
                        out=t1c[:], in_=x1stg[i_][:], axis=AX.X))(ds),
                        sig=True)
                    E.op("v", (lambda tb_: lambda e: e.tensor_add(
                        out=stats[:, tb_ * 2:tb_ * 2 + 1],
                        in0=stats[:, tb_ * 2:tb_ * 2 + 1], in1=t1c[:]))(tb),
                        waits=[r1], force=True)
                    sq = E.op("s", (lambda i_: lambda e: e.activation(
                        out=dumpB[:], in_=x1stg[i_][:], func=AF.Square,
                        accum_out=t2c[:]))(ds), waits=[ev], sig=True)
                    a2 = E.op("v", (lambda tb_: lambda e: e.tensor_add(
                        out=stats[:, tb_ * 2 + 1:tb_ * 2 + 2],
                        in0=stats[:, tb_ * 2 + 1:tb_ * 2 + 2], in1=t2c[:]))(tb),
                        waits=[sq], sig=True)
                    st = E.op("y", (lambda tb_, g_, ds_, i_: lambda e:
                                    e.dma_start(
                        out=dx1.ap()[tb_ * 128:(tb_ + 1) * 128,
                                     g_ * 1024 + ds_ * 512:
                                     g_ * 1024 + ds_ * 512 + 512],
                        in_=x1stg[i_][:]))(tb, g, ds, ds),
                        waits=[ev, a2], sig=True, dma=True,
                        dma_sem=f"x1s{ds}")
                    tok[f"x1s_free{ds}"] = st
                    x1_store[(g, tb, ds)] = st
        prev = None
        prev_store = tok.get("xns_last")
        for tb in range(NT):
            lx = E.op("y", (lambda tb_: lambda e: e.dma_start(
                out=xtile[:], in_=dx1.ap()[tb_ * 128:(tb_ + 1) * 128, :]))(tb),
                waits=[x1_store[(3, NT - 1, 1)], x1_store[(3, NT - 1, 0)],
                       prev], sig=True, dma=True, dma_sem="xt")
            E.op("v", (lambda tb_: lambda e: e.tensor_scalar(
                out=t1c[:], in0=stats[:, tb_ * 2:tb_ * 2 + 1],
                scalar1=1.0 / D, scalar2=None, op0=ALU.mult))(tb),
                waits=[x1_store[(3, tb, 1)]])
            E.op("v", (lambda tb_: lambda e: e.tensor_scalar(
                out=t2c[:], in0=stats[:, tb_ * 2 + 1:tb_ * 2 + 2],
                scalar1=1.0 / D, scalar2=None, op0=ALU.mult))(tb))
            nv = E.op("v", lambda e: e.scalar_tensor_tensor(
                out=s1c[:], in0=t1c[:], scalar=t1c[:], in1=t2c[:],
                op0=ALU.mult, op1=ALU.subtract), sig=True)
            lg = E.op("s", lambda e: e.activation(out=t2c[:], in_=s1c[:],
                                                  func=AF.Ln, scale=-1.0,
                                                  bias=epscol[:]),
                      waits=[nv], sig=True)
            rst = E.op("s", lambda e: e.activation(out=rstdc[:], in_=t2c[:],
                                                   func=AF.Exp, scale=-0.5,
                                                   bias=zcol[:]),
                      waits=[lg], sig=True, force=True)
            E.op("v", lambda e: e.tensor_scalar(
                out=t2c[:], in0=t1c[:], scalar1=rstdc[:], scalar2=None,
                op0=ALU.mult), waits=[rst])
            nm = E.op("v", lambda e: e.tensor_scalar(
                out=nmrc[:], in0=t2c[:], scalar1=-1.0, scalar2=None,
                op0=ALU.mult), sig=True)
            norm = E.op("s", lambda e: e.activation(
                out=xnst[:], in_=xtile[:], func=AF.Identity, scale=rstdc[:],
                bias=nmrc[:]), waits=[nm, lx, rst, prev_store], sig=True,
                force=True)
            prev = E.op("y", (lambda tb_: lambda e: e.dma_start(
                out=dxn2.ap()[tb_ * 128:(tb_ + 1) * 128, :], in_=xnst[:]))(tb),
                waits=[norm], sig=True, dma=True, dma_sem="xns")
            prev_store = prev
            tok[f"xn2_store{tb}"] = prev
        tok["passb_done"] = prev

    phase_passb()
    E.flush()
    gb.close()

    if not do_ffn:
        E.op("y", lambda e: e.nop(), waits=[tok["passb_done"]], force=True)
        E.flush()
        return nc

    gc_ = ExitStack()
    AwT = sb(gc_, "AwT", [128, KD * 512], BF16)
    BwT = sb(gc_, "BwT", [128, 4 * 4096], BF16)
    WqT = sb(gc_, "WqT", [128, KD * 16], BF16)
    keysT = sb(gc_, "keysT", [16, 64], BF16)
    Asb = sb(gc_, "Asb", [128, 4096], BF16)
    Bsb = sb(gc_, "Bsb", [128, 4 * 512], BF16)
    Wqsb = sb(gc_, "Wqsb", [16, 1024], BF16)
    keysb = sb(gc_, "keysb", [64, 16], BF16)
    xn2p = sb(gc_, "xn2p", [128, KD * 512], BF16)
    zT = sb(gc_, "zT", [128, 4 * 2048], BF16)
    qT = sb(gc_, "qT", [16, 2048], BF16)
    gateT = sb(gc_, "gateT", [64, NT * 128], BF16)
    sctile = sb(gc_, "sctile", [128, 64], F32)
    wk1 = sb(gc_, "wk1", [128, 64], F32)
    wk2 = sb(gc_, "wk2", [128, 64], F32)
    mx8 = sb(gc_, "mx8", [128, 8], F32)
    mx8b = sb(gc_, "mx8b", [128, 8], F32)
    gtile = sb(gc_, "gtile", [128, 64], F32)
    cEXP8 = sb(gc_, "cEXP8", [64, 512], BF16)
    x1b = [sb(gc_, f"x1b{i}", [128, D], F32) for i in range(2)]
    outstg = [sb(gc_, f"outstg{i}", [128, 512], F32) for i in range(2)]

    def phase_ffn():
        PBD = tok["passb_done"]
        E8L = E.op("y", lambda e: e.dma_start(out=cEXP8[:], in_=dEXP8.ap()),
                   waits=[PBD], sig=True, dma=True, dma_sem="e8l")
        atok = None
        for rg in range(4):
            la = E.op("g", (lambda rg_: lambda e: e.dma_start(
                out=Asb[:], in_=dAw.ap()[rg_ * 128:(rg_ + 1) * 128, :]))(rg),
                waits=[PBD, tok.get("Asb_free")], sig=True, dma=True,
                dma_sem="asb")
            for k in range(KD):
                m = E.op("t", (lambda k_: lambda e: e.matmul(
                    out=PS[6][:, 0:128], lhsT=Asb[:, k_ * 128:(k_ + 1) * 128],
                    rhs=cIDbf[:], start=True, stop=True))(k),
                    waits=[la, bank_r[6]], sig=True)
                bank_w[6] = m
                atok = E.op("v", (lambda k_, rg_: lambda e: e.tensor_copy(
                    out=AwT[:, k_ * 512 + rg_ * 128:
                            k_ * 512 + rg_ * 128 + 128],
                    in_=PS[6][:, 0:128]))(k, rg), waits=[m], sig=True)
                bank_r[6] = atok
            tok["Asb_free"] = atok
        btok = None
        for rg in range(8):
            lb = E.op("g", (lambda rg_: lambda e: e.dma_start(
                out=Bsb[:].rearrange("p (k c) -> p k c", k=4),
                in_=dBw.ap()[rg_ * 512:(rg_ + 1) * 512, :].rearrange(
                    "(k p) c -> p k c", p=128)))(rg),
                waits=[PBD, tok.get("Bsb_free")], sig=True, dma=True,
                dma_sem="bsb")
            for ko in range(4):
                for kn in range(4):
                    m = E.op("t", (lambda ko_, kn_: lambda e: e.matmul(
                        out=PS[7][:, 0:128],
                        lhsT=Bsb[:, ko_ * 512 + kn_ * 128:
                                 ko_ * 512 + kn_ * 128 + 128],
                        rhs=cIDbf[:], start=True, stop=True))(ko, kn),
                        waits=[lb, bank_r[7]], sig=True)
                    bank_w[7] = m
                    btok = E.op("v", (lambda rg_, ko_, kn_: lambda e:
                                      e.tensor_copy(
                        out=BwT[:, kn_ * 4096 + rg_ * 512 + ko_ * 128:
                                kn_ * 4096 + rg_ * 512 + ko_ * 128 + 128],
                        in_=PS[7][:, 0:128]))(rg, ko, kn), waits=[m],
                        sig=True)
                    bank_r[7] = btok
            tok["Bsb_free"] = btok
        lk = E.op("g", lambda e: e.dma_start(out=keysb[:], in_=dkeys.ap()),
                  waits=[PBD], sig=True, dma=True, dma_sem="keysl")
        qts = None
        for quarter in range(4):
            lq = E.op("g", (lambda q_: lambda e: e.dma_start(
                out=Wqsb[:], in_=dWq.ap()[:, q_ * 1024:(q_ + 1) * 1024]))(quarter),
                waits=[PBD, qts], sig=True, dma=True, dma_sem="wq")
            for kk in range(8):
                k = quarter * 8 + kk
                m = E.op("t", (lambda kk_: lambda e: e.matmul(
                    out=PS[6][:, 128 + kk_ * 16: 128 + kk_ * 16 + 16],
                    lhsT=Wqsb[:, kk_ * 128:(kk_ + 1) * 128], rhs=cI16[:],
                    start=True, stop=True))(kk),
                    waits=[lq] + ([bank_r[6]] if kk == 0 else []), sig=True)
                bank_w[6] = m
                if kk == 7:
                    qts = E.op("v", (lambda k_: lambda e: e.tensor_copy(
                        out=WqT[:, (k_ - 7) * 16:(k_ + 1) * 16],
                        in_=PS[6][:, 128:256]))(k), waits=[m], sig=True)
                    bank_r[6] = qts
        mk_ = E.op("t", lambda e: e.matmul(
            out=PS[7][0:16, 128:192], lhsT=keysb[:], rhs=cI64bf[:],
            start=True, stop=True), waits=[lk, bank_r[7]], sig=True)
        bank_w[7] = mk_
        kev = E.op("v", lambda e: e.tensor_copy(out=keysT[:],
                                                in_=PS[7][0:16, 128:192]),
                   waits=[mk_], sig=True)
        bank_r[7] = kev

        ztoks = {}
        qtoks = {}
        for tg in range(4):
            pl = None
            for k in range(KD):
                pl = E.op("s", (lambda tg_, k_: lambda e: e.dma_start(
                    out=xn2p[:, k_ * 512:(k_ + 1) * 512],
                    in_=dxn2.ap()[tg_ * 512:(tg_ + 1) * 512,
                                  k_ * 128:(k_ + 1) * 128],
                    transpose=True))(tg, k),
                    waits=[tok["passb_done"],
                           tok.get("xn2p_free")], sig=True, dma=True,
                    dma_sem="xn2p")
            mm = None
            for k in range(KD):
                mm = E.op("t", (lambda k_, s_, p_: lambda e: e.matmul(
                    out=PS[4][0:16, 0:512], lhsT=WqT[:, k_ * 16:(k_ + 1) * 16],
                    rhs=xn2p[:, k_ * 512:(k_ + 1) * 512], start=s_,
                    stop=p_))(k, k == 0, k == KD - 1),
                    waits=[pl, qts, bank_r[4]] if k == 0 else (),
                    sig=(k == KD - 1))
            bank_w[4] = mm
            qe = E.op("v", (lambda tg_: lambda e: e.tensor_copy(
                out=qT[:, tg_ * 512:(tg_ + 1) * 512],
                in_=PS[4][0:16, 0:512]))(tg), waits=[mm], sig=True)
            bank_r[4] = qe
            qtoks[tg] = qe
            for nb in range(4):
                pb = nb % 2
                mm = None
                for k in range(KD):
                    mm = E.op("t", (lambda k_, nb_, pb_, s_, p_: lambda e:
                                    e.matmul(
                        out=PS[pb_][:, 0:512],
                        lhsT=AwT[:, k_ * 512 + nb_ * 128:
                                 k_ * 512 + nb_ * 128 + 128],
                        rhs=xn2p[:, k_ * 512:(k_ + 1) * 512], start=s_,
                        stop=p_))(k, nb, pb, k == 0, k == KD - 1),
                        waits=[pl, atok, bank_r[pb]] if k == 0 else (),
                        sig=(k == KD - 1))
                bank_w[pb] = mm
                ze = E.op("v", (lambda nb_, tg_, pb_: lambda e: e.tensor_copy(
                    out=zT[:, nb_ * 2048 + tg_ * 512:
                           nb_ * 2048 + tg_ * 512 + 512],
                    in_=PS[pb_][:, 0:512]))(nb, tg, pb), waits=[mm], sig=True)
                bank_r[pb] = ze
                ztoks[(tg, nb)] = ze
                if nb == 3:
                    tok["xn2p_free"] = mm

        gttoks = {}
        for tb in range(NT):
            mm = E.op("t", (lambda tb_: lambda e: e.matmul(
                out=PS[4][:, 0:64], lhsT=qT[:, tb_ * 128:(tb_ + 1) * 128],
                rhs=keysT[:], start=True, stop=True))(tb),
                waits=[qtoks[tb // 4], kev, bank_r[4]], sig=True)
            bank_w[4] = mm
            sc = E.op("v", lambda e: e.tensor_copy(out=sctile[:],
                                                   in_=PS[4][:, 0:64]),
                      waits=[mm], sig=True)
            bank_r[4] = sc
            a = E.op("v", lambda e: e.max(out=mx8[:], in_=sctile[:]),
                     waits=[sc], sig=True)
            b = E.op("v", lambda e: e.match_replace(
                out=wk1[:], in_to_replace=mx8[:], in_values=sctile[:],
                imm_value=-1e30), waits=[a], sig=True, force=True)
            c = E.op("v", lambda e: e.max(out=mx8b[:], in_=wk1[:]),
                     waits=[b], sig=True, force=True)
            d = E.op("v", lambda e: e.match_replace(
                out=wk2[:], in_to_replace=mx8b[:], in_values=wk1[:],
                imm_value=-1e30), waits=[c], sig=True, force=True)
            nm1 = E.op("v", lambda e: e.tensor_scalar(
                out=t1c[:], in0=mx8[:, 0:1], scalar1=-1.0, scalar2=None,
                op0=ALU.mult), waits=[d], sig=True, force=True)
            ex = E.op("s", lambda e: e.activation(
                out=wk1[:], in_=sctile[:], func=AF.Exp, scale=1.0,
                bias=t1c[:]), waits=[nm1], sig=True)
            msk = E.op("v", lambda e: e.tensor_tensor(
                out=wk2[:], in0=sctile[:], in1=wk2[:], op=ALU.not_equal),
                waits=[ex], sig=True, force=True)
            gmm = E.op("v", lambda e: e.tensor_mul(out=wk1[:], in0=wk1[:],
                                                   in1=wk2[:]),
                       waits=[msk], sig=True, force=True)
            dn = E.op("v", lambda e: e.reduce_sum(out=t2c[:], in_=wk1[:],
                                                  axis=AX.X),
                      waits=[gmm], sig=True, force=True)
            rd = E.op("v", lambda e: e.reciprocal(out=t1c[:], in_=t2c[:]),
                      waits=[dn], sig=True, force=True)
            gt = E.op("v", lambda e: e.tensor_scalar(
                out=gtile[:], in0=wk1[:], scalar1=t1c[:], scalar2=None,
                op0=ALU.mult), waits=[rd], sig=True, force=True)
            gm2 = E.op("t", lambda e: e.matmul(
                out=PS[5][0:64, 0:128], lhsT=gtile[:], rhs=cIDf[:],
                start=True, stop=True), waits=[gt, bank_r[5]], sig=True)
            bank_w[5] = gm2
            ge = E.op("v", (lambda tb_: lambda e: e.tensor_copy(
                out=gateT[:, tb_ * 128:(tb_ + 1) * 128],
                in_=PS[5][0:64, 0:128]))(tb), waits=[gm2], sig=True)
            bank_r[5] = ge
            gttoks[tb] = ge

        zgtoks = {}
        for tg in range(4):
            for nb in range(4):
                pb = 2 + (nb % 2)
                mm = E.op("t", (lambda tg_, nb_, pb_: lambda e: e.matmul(
                    out=PS[pb_][:, 0:512],
                    lhsT=cEXP8[:, nb_ * 128:(nb_ + 1) * 128],
                    rhs=gateT[:, tg_ * 512:(tg_ + 1) * 512], start=True,
                    stop=True))(tg, nb, pb),
                    waits=[gttoks[tg * 4 + 3], E8L, bank_r[pb]], sig=True)
                bank_w[pb] = mm
                zg = E.op("v", (lambda tg_, nb_, pb_: lambda e: e.tensor_mul(
                    out=zT[:, nb_ * 2048 + tg_ * 512:
                           nb_ * 2048 + tg_ * 512 + 512],
                    in0=zT[:, nb_ * 2048 + tg_ * 512:
                           nb_ * 2048 + tg_ * 512 + 512],
                    in1=PS[pb_][:, 0:512]))(tg, nb, pb),
                    waits=[mm, ztoks[(tg, nb)]], sig=True)
                bank_r[pb] = zg
                zgtoks[(tg, nb)] = zg
        ZGL = zgtoks[(3, 3)]

        out_last = None
        for tb in range(NT):
            xl = E.op("y", (lambda tb_: lambda e: e.dma_start(
                out=x1b[tb_ % 2][:],
                in_=dx1.ap()[tb_ * 128:(tb_ + 1) * 128, :]))(tb),
                waits=[x1_store[(3, NT - 1, 1)], x1_store[(3, NT - 1, 0)],
                       tok.get(f"x1b_free{tb % 2}")],
                sig=True, dma=True, dma_sem=f"x1b{tb % 2}")
            for og in range(8):
                pb = og % 2
                mm = None
                for kn in range(4):
                    mm = E.op("t", (lambda tb_, og_, kn_, pb_, s_, p_:
                                    lambda e: e.matmul(
                        out=PS[pb_][:, 0:512],
                        lhsT=zT[:, kn_ * 2048 + tb_ * 128:
                                kn_ * 2048 + tb_ * 128 + 128],
                        rhs=BwT[:, kn_ * 4096 + og_ * 512:
                                kn_ * 4096 + og_ * 512 + 512],
                        start=s_, stop=p_))(tb, og, kn, pb, kn == 0, kn == 3),
                        waits=[ZGL, btok, bank_r[pb]] if kn == 0 else (),
                        sig=(kn == 3))
                bank_w[pb] = mm
                fv = E.op("v", (lambda tb_, og_, pb_, i_: lambda e:
                                e.scalar_tensor_tensor(
                    out=outstg[i_][:], in0=PS[pb_][:, 0:512], scalar=2.0,
                    in1=x1b[tb_ % 2][:, og_ * 512: og_ * 512 + 512],
                    op0=ALU.mult, op1=ALU.add))(tb, og, pb, og % 2),
                    waits=[mm, xl, tok.get(f"outst_free{og % 2}")], sig=True)
                bank_r[pb] = fv
                if og == 7:
                    tok[f"x1b_free{tb % 2}"] = fv
                out_last = E.op("y", (lambda tb_, og_, i_: lambda e:
                                      e.dma_start(
                    out=dout.ap()[tb_ * 128:(tb_ + 1) * 128,
                                  og_ * 512: og_ * 512 + 512],
                    in_=outstg[i_][:]))(tb, og, og % 2),
                    waits=[fv], sig=True, dma=True,
                    dma_sem=f"outst{og % 2}")
                tok[f"outst_free{og % 2}"] = out_last
        E.op("y", lambda e: e.nop(),
             waits=[tok["outst_free0"], tok["outst_free1"]], force=True)

    phase_ffn()
    if debug:
        for nm_, buf in [("rscale", rscale), ("stats", stats),
                         ("gateT", gateT), ("qT", qT), ("zT", zT),
                         ("sctile", sctile), ("gtile", gtile), ("wk1", wk1),
                         ("wk2", wk2), ("mx8", mx8), ("mx8b", mx8b)]:
            dt_ = nc.dram_tensor("dbgf_" + nm_, list(buf[:].shape),
                                 buf[:].dtype, kind="ExternalOutput")
            E.op("y", (lambda b_, d_: lambda e: e.dma_start(
                out=d_.ap(), in_=b_[:]))(buf, dt_),
                waits=[tok.get("outst_free0"), tok.get("outst_free1")],
                sig=True, dma=True, dma_sem="dbg")
    E.flush()
    gc_.close()
    return nc


from concourse.bass_utils import run_bass_kernel_spmd

_BUILT = None

def kernel(**inputs):
    global _BUILT
    in_maps = [host_arrays(inputs, core) for core in range(8)]
    if _BUILT is None:
        _BUILT = build(debug=False)
    nc = _BUILT
    res = run_bass_kernel_spmd(nc, in_maps, core_ids=list(range(8)))
    out = np.stack([np.asarray(res.results[i]["out"]) for i in range(8)], axis=0)
    return out.astype(np.float32)


# version 07 (final)
# speedup vs baseline: 1.2500x; 1.2500x over previous
import numpy as np
import ml_dtypes
from contextlib import ExitStack
import concourse.bass as bass
import concourse.mybir as mybir

F32 = mybir.dt.float32
BF16 = mybir.dt.bfloat16
AF = mybir.ActivationFunctionType
ALU = mybir.AluOpType
AX = mybir.AxisListType

T, TH, D, KD = 2048, 1024, 4096, 32
NT, NTH, NH, Q, NCH = 16, 8, 64, 128, 8
DSSM, PROJ, EPS = 8192, 16704, 1e-5

nbf = lambda a: np.asarray(a, dtype=ml_dtypes.bfloat16)


def host_arrays(inp, core):
    x = np.ascontiguousarray(np.asarray(inp["L0_attn_hidden_states"], dtype=np.float32)[core])
    conv_w = np.asarray(inp["L0_attn_conv_w"], dtype=np.float32)
    d = {
        "x": x,
        "W_in": np.ascontiguousarray(np.asarray(inp["L0_attn_W_in"], dtype=np.float32)),
        "W_out": np.ascontiguousarray(np.asarray(inp["L0_attn_W_out"], dtype=np.float32)),
        "conv_pk": np.ascontiguousarray(
            conv_w.reshape(66, 128, 4).transpose(1, 0, 2).reshape(128, 264)),
        "dt_bias_c": np.ascontiguousarray(np.asarray(inp["L0_attn_dt_bias"], dtype=np.float32).reshape(64, 1)),
        "A_log_c": np.ascontiguousarray(np.asarray(inp["L0_attn_A_log"], dtype=np.float32).reshape(64, 1)),
        "D_row": np.ascontiguousarray(np.asarray(inp["L0_attn_D_param"], dtype=np.float32).reshape(1, 64)),
        "nw_rows": np.ascontiguousarray(np.asarray(inp["L0_attn_norm_weight"], dtype=np.float32).reshape(64, 128)),
        "A_w": np.ascontiguousarray(np.asarray(inp["L0_ffn_A_w"], dtype=np.float32)),
        "B_w": np.ascontiguousarray(np.asarray(inp["L0_ffn_B_w"], dtype=np.float32)),
        "Wq_w": np.ascontiguousarray(np.asarray(inp["L0_ffn_Wq_w"], dtype=np.float32)),
        "keys": np.ascontiguousarray(np.asarray(inp["L0_ffn_keys"], dtype=np.float32)),
        "IDbf": nbf(np.eye(128, dtype=np.float32)),
        "IDf32": np.eye(128, dtype=np.float32),
        "I64f": np.eye(64, dtype=np.float32),
        "I16bf": nbf(np.eye(16, dtype=np.float32)),
        "SLT": np.tril(np.ones((128, 128), dtype=np.float32), -1),
        "MASKNEG": np.tril(np.ones((128, 128), dtype=np.float32), -1) * np.float32(-1e30),
        "UT025": np.triu(np.ones((128, 128), dtype=np.float32), 0) * np.float32(0.25),
        "ONEROW127": np.zeros((128, 128), dtype=np.float32),
        "SCANMASK": np.ones((64, TH), dtype=np.float32),
        "EXP8": nbf(np.ascontiguousarray(
            (np.arange(512)[None, :] // 8 == np.arange(64)[:, None]).astype(np.float32))),
    }
    d["ONEROW127"][127, :] = 1.0
    d["SCANMASK"][:, 0::Q] = 0.0
    return d


ENGS = ("t", "v", "s", "y", "g")


class Em:
    def __init__(self, nc, es):
        self.nc = nc
        self.es = es
        self.ops = {e: [] for e in ENGS}
        self.grp = -1
        self.sem = {}
        self.cnt = {}
        self.waited = {e: {} for e in ENGS}
        self.last_v = None
        self.new_group()

    def new_group(self):
        self.grp += 1
        self.last_v = None
        for e in ENGS:
            h = self.es.enter_context(self.nc.semaphore(f"tok_{e}{self.grp}"))
            self.sem[e] = h
            self.cnt[e] = 0
        self.dsem = {}
        self.dcnt = {}

    def _dma_sem(self, name):
        if name not in self.dsem:
            self.dsem[name] = self.es.enter_context(
                self.nc.semaphore(f"dma_{name}{self.grp}"))
            self.dcnt[name] = 0
        return self.dsem[name]

    def op(self, eng, fn, waits=(), sig=False, dma=False, force=False,
           dma_sem=None):
        if dma:
            assert dma_sem is not None, "DMA ops need a named class semaphore"
            mysem = self._dma_sem(dma_sem)
        else:
            mysem = self.sem[eng]
        if eng == "v" and not dma:
            sig = True
            force = True
        wl = []
        for w in waits:
            if w is None:
                continue
            sem, val = w
            if sem is mysem and not force:
                continue
            key = id(sem)
            prev = self.waited[eng].get(key, -1)
            if val <= prev:
                continue
            self.waited[eng][key] = val
            wl.append((sem, val))
        tokn = None
        if sig:
            if dma:
                self.dcnt[dma_sem] += 16
                tokn = (mysem, self.dcnt[dma_sem])
            else:
                self.cnt[eng] += 1
                tokn = (mysem, self.cnt[eng])
        self.ops[eng].append((fn, wl, (mysem, 16 if dma else 1) if sig else None))
        if eng == "v" and not dma:
            self.last_v = tokn
        return tokn

    def flush(self):
        ops = self.ops
        self.ops = {e: [] for e in ENGS}

        def run(lst, engobj):
            for fn, wl, sig in lst:
                for sem, val in wl:
                    engobj.wait_ge(sem, val)
                instr = fn(engobj)
                if sig is not None:
                    sem, inc = sig
                    instr.then_inc(sem, inc)

        with self.nc.Block() as blk:
            if ops["t"]:
                @blk.tensor
                def _(e):
                    run(ops["t"], e)
            if ops["v"]:
                @blk.vector
                def _(e):
                    run(ops["v"], e)
            if ops["s"]:
                @blk.scalar
                def _(e):
                    run(ops["s"], e)
            if ops["y"]:
                @blk.sync
                def _(e):
                    run(ops["y"], e)
            if ops["g"]:
                @blk.gpsimd
                def _(e):
                    run(ops["g"], e)


def build(debug=False, max_heads=NH, halves=(0, 1), do_z=True,
          do_passb=True, do_ffn=True):
    nc = bass.Bass("TRN2", target_bir_lowering=False, debug=False)
    es = ExitStack()

    def din(name, shape, dt=F32):
        return nc.dram_tensor(name, list(shape), dt, kind="ExternalInput")

    dx = din("x", [T, D])
    dWin = din("W_in", [D, PROJ])
    dWout = din("W_out", [DSSM, D])
    dconv = din("conv_pk", [128, 264])
    ddtb = din("dt_bias_c", [64, 1])
    dAlog = din("A_log_c", [64, 1])
    dDrow = din("D_row", [1, 64])
    dnw = din("nw_rows", [64, 128])
    dAw = din("A_w", [512, D])
    dBw = din("B_w", [D, 512])
    dWq = din("Wq_w", [16, D])
    dkeys = din("keys", [64, 16])
    dIDbf = din("IDbf", [128, 128], BF16)
    dIDf = din("IDf32", [128, 128])
    dI64 = din("I64f", [64, 64])
    dI16 = din("I16bf", [16, 16], BF16)
    dSLT = din("SLT", [128, 128])
    dMASK = din("MASKNEG", [128, 128])
    dUT025 = din("UT025", [128, 128])
    dOR127 = din("ONEROW127", [128, 128])
    dSCM = din("SCANMASK", [64, TH])
    dEXP8 = din("EXP8", [64, 512], BF16)

    dout = nc.dram_tensor("out", [T, D], F32, kind="ExternalOutput")
    skind = "ExternalOutput" if debug else "Internal"
    dxn = nc.dram_tensor("xn_hbm", [T, D], BF16, kind=skind)
    dsz = nc.dram_tensor("sz_hbm", [T, DSSM], BF16, kind=skind)
    dygT = nc.dram_tensor("ygT_hbm", [NT * NH * 128, 128], BF16, kind=skind)
    dx1 = nc.dram_tensor("x1_hbm", [T, D], F32, kind=skind)
    dxn2 = nc.dram_tensor("xn2_hbm", [T, D], BF16, kind=skind)
    dWbf = nc.dram_tensor("wbf_hbm", [66 * 128, KD * 256], BF16, kind="Internal")

    def sb(stack, name, shape, dtype):
        return stack.enter_context(nc.sbuf_tensor(name, shape, dtype))

    cIDbf = sb(es, "cIDbf", [128, 128], BF16)
    cIDf = sb(es, "cIDf", [128, 128], F32)
    cI64 = sb(es, "cI64", [64, 64], F32)
    cI16 = sb(es, "cI16", [16, 16], BF16)
    cI64bf = sb(es, "cI64bf", [64, 64], BF16)
    cSLT = sb(es, "cSLT", [128, 128], F32)
    cMASK = sb(es, "cMASK", [128, 128], F32)
    cUT025 = sb(es, "cUT025", [128, 128], F32)
    cOR127 = sb(es, "cOR127", [128, 128], F32)
    cSCM = sb(es, "cSCM", [64, TH], F32)
    cconv = sb(es, "cconv", [128, 264], F32)
    cdtb = sb(es, "cdtb", [64, 1], F32)
    cAlog = sb(es, "cAlog", [64, 1], F32)
    cDrow = sb(es, "cDrow", [1, 64], F32)
    cnw = sb(es, "cnw", [64, 128], F32)
    cnwT = sb(es, "cnwT", [128, 64], F32)
    cDrep = sb(es, "cDrep", [128, 64], F32)
    zcol = sb(es, "zcol", [128, 1], F32)
    onecol = sb(es, "onecol", [128, 1], F32)
    epscol = sb(es, "epscol", [128, 1], F32)
    eps4col = sb(es, "eps4col", [128, 1], F32)
    ln05col = sb(es, "ln05col", [128, 1], F32)
    negEA = sb(es, "negEA", [64, 1], F32)
    ones1f = sb(es, "ones1f", [1, 128], F32)
    xtile = sb(es, "xtile", [128, D], F32)
    xnst = sb(es, "xnst", [128, D], BF16)
    s1c = sb(es, "s1c", [128, 1], F32)
    s2c = sb(es, "s2c", [128, 1], F32)
    t1c = sb(es, "t1c", [128, 1], F32)
    t2c = sb(es, "t2c", [128, 1], F32)
    rstdc = sb(es, "rstdc", [128, 1], F32)
    nmrc = sb(es, "nmrc", [128, 1], F32)
    stats = sb(es, "stats", [128, NT * 2], F32)
    rscale = sb(es, "rscale", [128, NT], F32)
    PS = [es.enter_context(nc.psum_tensor(f"ps{i}", [128, 512], F32))
          for i in range(8)]

    E = Em(nc, es)
    tok = {}
    bank_r = [None] * 8
    bank_w = [None] * 8

    LD = None
    for dram, tile in [(dIDbf, cIDbf), (dIDf, cIDf), (dI64, cI64),
                       (dI16, cI16), (dSLT, cSLT), (dMASK, cMASK),
                       (dUT025, cUT025), (dOR127, cOR127), (dSCM, cSCM),
                       (dconv, cconv), (ddtb, cdtb),
                       (dAlog, cAlog), (dDrow, cDrow), (dnw, cnw)]:
        LD = E.op("y", (lambda d_, t_: lambda e: e.dma_start(
            out=t_[:], in_=d_.ap()))(dram, tile), sig=True, dma=True, dma_sem="cst")

    E.op("v", lambda e: e.memset(zcol[:], 0.0))
    E.op("v", lambda e: e.memset(onecol[:], 1.0))
    E.op("v", lambda e: e.memset(epscol[:], EPS))
    E.op("v", lambda e: e.memset(eps4col[:], 4.0 * EPS))
    E.op("v", lambda e: e.memset(ln05col[:], float(np.log(0.5))))
    E.op("v", lambda e: e.memset(ones1f[:], 1.0))
    E.op("v", lambda e: e.memset(stats[:], 0.0))
    E.op("v", lambda e: e.tensor_copy(out=cI64bf[:], in_=cI64[:]), waits=[LD])
    VINIT = E.op("v", lambda e: e.memset(rscale[:], 0.0), sig=True)

    eA0 = E.op("s", lambda e: e.activation(out=negEA[:], in_=cAlog[:],
                                           func=AF.Exp, scale=1.0,
                                           bias=zcol[0:64, :]),
               waits=[LD, VINIT], sig=True)
    NEA = E.op("v", lambda e: e.tensor_scalar(out=negEA[:], in0=negEA[:],
                                              scalar1=-1.0, scalar2=None,
                                              op0=ALU.mult),
               waits=[eA0], sig=True)


    WRM = E.op("t", lambda e: e.matmul(out=PS[7][:, 0:128], lhsT=cSLT[:],
                                       rhs=cSLT[:], start=True, stop=True),
               waits=[LD, VINIT], sig=True)
    E.op("t", lambda e: e.matmul(out=PS[6][:, 0:64], lhsT=cnw[:], rhs=cI64[:],
                                 start=True, stop=True))
    T0 = E.op("t", lambda e: e.matmul(out=PS[6][:, 64:128], lhsT=ones1f[:],
                                      rhs=cDrow[:], start=True, stop=True),
              sig=True)
    E.op("v", lambda e: e.tensor_copy(out=cnwT[:], in_=PS[6][:, 0:64]),
         waits=[T0])
    CDR = E.op("v", lambda e: e.tensor_copy(out=cDrep[:], in_=PS[6][:, 64:128]),
               sig=True)
    bank_r[6] = CDR
    bank_r[7] = WRM

    ga = ExitStack()
    xnT = sb(ga, "xnT", [128, KD * TH], BF16)
    WCH = [sb(ga, f"WCH{i}", [128, KD * 256], BF16) for i in range(2)]
    Bbf = sb(ga, "Bbf", [128, TH], BF16)
    Cbf = sb(ga, "Cbf", [128, TH], BF16)
    BTt = sb(ga, "BTt", [128, NCH * 128], BF16)
    Gm = sb(ga, "Gm", [128, NCH * 128], BF16)
    cumT = sb(ga, "cumT", [128, NCH * 64], F32)
    dtTh = sb(ga, "dtTh", [128, NCH * 64], F32)
    dtbh = sb(ga, "dtbh", [128, NCH * 64], F32)
    DdtB = sb(ga, "DdtB", [128, NCH * 64], F32)
    AtotR = sb(ga, "AtotR", [128, NCH * 64], F32)
    cum_ht = sb(ga, "cum_ht", [64, TH], F32)
    dtF = sb(ga, "dtF", [64, TH], F32)
    logdA = sb(ga, "logdA", [64, TH], F32)
    sm64b = sb(ga, "sm64b", [128, 64], F32)
    sm64c = sb(ga, "sm64c", [128, 64], F32)
    bTbuf = sb(ga, "bTbuf", [128, 64], F32)
    tails = sb(ga, "tails", [128, 66 * 3], F32)
    raw_ext = sb(ga, "raw_ext", [128, TH + 4], BF16)
    acc_cv = sb(ga, "acc_cv", [128, TH], BF16)
    ubuf = sb(ga, "ubuf", [128, TH], BF16)
    xhbf = [sb(ga, f"xhbf{i}", [128, TH], BF16) for i in range(2)]
    szt = [sb(ga, f"szt{i}", [128, TH], BF16) for i in range(2)]
    Hbf = sb(ga, "Hbf", [128, NH * 128], BF16)
    dtxT = [[sb(ga, f"dtxT{i}_{j}", [128, 128], BF16) for j in range(2)]
            for i in range(2)]
    DtXb = [[sb(ga, f"DtXb{i}_{j}", [128, 128], BF16) for j in range(2)]
            for i in range(2)]
    Mske = [sb(ga, f"Mske{i}", [128, 128], F32) for i in range(2)]
    LTb = [sb(ga, f"LTb{i}", [128, 128], BF16) for i in range(2)]
    aBrd = [sb(ga, f"aBrd{i}", [128, 128], BF16) for i in range(2)]
    aCb = [[sb(ga, f"aCb{i}_{j}", [128, 128], BF16) for j in range(2)]
           for i in range(2)]
    STb = [[sb(ga, f"STb{i}_{j}", [128, 128], BF16) for j in range(2)]
           for i in range(2)]
    yDb = [sb(ga, f"yDb{i}", [128, 128], BF16) for i in range(2)]
    ygS = [sb(ga, f"ygS{i}", [128, 128], BF16) for i in range(2)]
    ygTS = [sb(ga, f"ygTS{i}", [128, 128], BF16) for i in range(2)]
    rmsb = sb(ga, "rmsb", [128, NT * 64], F32)
    dump = sb(ga, "dump", [128, 128], BF16)
    szstg = [sb(ga, f"szstg{i}", [128, 256], BF16) for i in range(2)]

    wmx = rmsb[:, 0:8]
    wmw = rmsb[:, 8:136]
    E.op("v", lambda e: e.max(out=wmx, in_=cSLT[:]), waits=[LD])
    E.op("v", lambda e: e.match_replace(out=wmw, in_to_replace=wmx,
                                        in_values=cSLT[:], imm_value=-1e30))
    E.op("v", lambda e: e.max(out=wmx, in_=wmw))
    E.op("v", lambda e: e.match_replace(out=wmw, in_to_replace=wmx,
                                        in_values=wmw, imm_value=-1e30))
    E.op("v", lambda e: e.reduce_sum(out=rmsb[:, 0:1], in_=wmw, axis=AX.X))
    E.op("v", lambda e: e.reciprocal(out=rmsb[:, 0:1], in_=onecol[:]))
    HINIT = E.op("v", lambda e: e.memset(Hbf[:], 0.0), sig=True)
    E.op("v", lambda e: e.memset(tails[:], 0.0))
    RMSI = E.op("v", lambda e: e.memset(rmsb[:], 0.0), sig=True)

    wch_tok = [None, None]
    wch_free = [None, None]
    wst_tok = [None, None]
    wslot = [0]

    def load_wchunk(c0, w, b, cached):
        slot = wslot[0]
        wslot[0] += 1
        war = wch_free[b]
        if cached:
            t_ = E.op("s", (lambda sl: lambda e: e.dma_start(
                out=WCH[b][:], in_=dWbf.ap()[sl * 128:(sl + 1) * 128, :]))(
                slot % 66),
                waits=[war, wst_tok[0], wst_tok[1], tok.get("wst_final")],
                sig=True, dma=True, dma_sem=f"wchc{b}")
            wch_tok[b] = t_
            return t_
        t_ = None
        for jj in range(4):
            src = dWin.ap()[jj * 8 * 128:(jj + 1) * 8 * 128, c0:c0 + w] \
                .rearrange("(k p) c -> p k c", p=128)
            dst = WCH[b][:, jj * 8 * 256:(jj + 1) * 8 * 256] \
                .rearrange("p (k c) -> p k c", k=8)[:, :, 0:w]
            t_ = E.op("g", (lambda s_, d_: lambda e: e.dma_start(
                out=d_, in_=s_))(src, dst), waits=[war, VINIT, wst_tok[b]],
                sig=True, dma=True, dma_sem=f"wch{b}")
        wch_tok[b] = t_
        wst_tok[b] = E.op("y", (lambda sl: lambda e: e.dma_start(
            out=dWbf.ap()[sl * 128:(sl + 1) * 128, :], in_=WCH[b][:]))(slot % 66),
            waits=[t_], sig=True, dma=True, dma_sem=f"wst{b}")
        tok["wst_final"] = wst_tok[b]
        return t_

    def phase_ln1(half):
        base = half * TH
        prev_norm = tok.get("ln_prev")
        last_store = tok.get("xns_last")
        for tb in range(NTH):
            lx = E.op("y", (lambda r0: lambda e: e.dma_start(
                out=xtile[:], in_=dx.ap()[r0:r0 + 128, :]))(base + tb * 128),
                waits=[prev_norm], sig=True, dma=True, dma_sem="xt")
            rs = E.op("v", lambda e: e.reduce_sum(out=s1c[:], in_=xtile[:],
                                                  axis=AX.X), waits=[lx],
                      sig=True)
            sq = E.op("s", lambda e: e.activation(out=xnst[:], in_=xtile[:],
                                                  func=AF.Square,
                                                  accum_out=s2c[:]),
                      waits=[lx], sig=True)
            m1t = E.op("v", lambda e: e.tensor_scalar(out=t1c[:], in0=s1c[:],
                                                      scalar1=1.0 / D,
                                                      scalar2=None,
                                                      op0=ALU.mult),
                       waits=[sq, rs], sig=True)
            m2t = E.op("v", lambda e: e.tensor_scalar(out=t2c[:], in0=s2c[:],
                                                      scalar1=1.0 / D,
                                                      scalar2=None,
                                                      op0=ALU.mult),
                       waits=[sq], sig=True)
            nv = E.op("v", lambda e: e.scalar_tensor_tensor(
                out=s1c[:], in0=t1c[:], scalar=t1c[:], in1=t2c[:],
                op0=ALU.mult, op1=ALU.subtract), waits=[m1t, m2t], sig=True)
            lg = E.op("s", lambda e: e.activation(out=t2c[:], in_=s1c[:],
                                                  func=AF.Ln, scale=-1.0,
                                                  bias=epscol[:]),
                      waits=[nv], sig=True)
            rst = E.op("s", lambda e: e.activation(out=rstdc[:], in_=t2c[:],
                                                   func=AF.Exp, scale=-0.5,
                                                   bias=zcol[:]),
                      waits=[lg], sig=True, force=True)
            p1t = E.op("v", lambda e: e.tensor_scalar(
                out=t2c[:], in0=t1c[:], scalar1=rstdc[:], scalar2=None,
                op0=ALU.mult), waits=[rst, m1t], sig=True)
            nm = E.op("v", lambda e: e.tensor_scalar(
                out=nmrc[:], in0=t2c[:], scalar1=-1.0, scalar2=None,
                op0=ALU.mult), waits=[p1t], sig=True)
            norm = E.op("s", lambda e: e.activation(out=xnst[:], in_=xtile[:],
                                                    func=AF.Identity,
                                                    scale=rstdc[:],
                                                    bias=nmrc[:]),
                        waits=[nm, rst, last_store], sig=True, force=True)
            prev_norm = norm
            last_store = E.op("y", (lambda r0: lambda e: e.dma_start(
                out=dxn.ap()[r0:r0 + 128, :], in_=xnst[:]))(base + tb * 128),
                waits=[norm], sig=True, dma=True, dma_sem="xns")
        tok["ln_prev"] = prev_norm
        tok["xns_last"] = last_store

        tr = None
        for k in range(KD):
            tr = E.op("s", (lambda k_: lambda e: e.dma_start(
                out=xnT[:, k_ * TH:(k_ + 1) * TH],
                in_=dxn.ap()[base:base + TH, k_ * 128:(k_ + 1) * 128],
                transpose=True))(k),
                waits=[last_store, tok.get("xnT_free")], sig=True, dma=True,
                dma_sem="xnt")
        tok["xnT_ready"] = tr

    def phase_bcd(half):
        cached = half != halves[0]
        lbc = load_wchunk(16384, 256, 0, cached)
        ldt = load_wchunk(16640, 64, 1, cached)
        XR = tok["xnT_ready"]

        raw_free = tok.get("raw_free")
        for cb, dstbuf, blkid in [(0, Bbf, 64), (1, Cbf, 65)]:
            evs = []
            for tg in range(2):
                pb = tg
                mm = None
                for k in range(KD):
                    mm = E.op("t", (lambda k_, tg_, cb_, s_, p_: lambda e:
                                    e.matmul(
                        out=PS[tg_][:, 0:512],
                        lhsT=WCH[0][:, k_ * 256 + cb_ * 128:
                                    k_ * 256 + cb_ * 128 + 128],
                        rhs=xnT[:, k_ * TH + tg_ * 512:
                                k_ * TH + tg_ * 512 + 512],
                        start=s_, stop=p_))(k, tg, cb, k == 0, k == KD - 1),
                        waits=[XR, lbc, bank_r[pb]] if k == 0 else (),
                        sig=(k == KD - 1))
                bank_w[pb] = mm
                ev = E.op("s", (lambda tg_: lambda e: e.copy(
                    out=raw_ext[:, 4 + tg_ * 512: 4 + tg_ * 512 + 512],
                    in_=PS[tg_][:, 0:512]))(tg),
                    waits=[mm, raw_free], sig=True)
                evs.append(ev)
                bank_r[pb] = ev
            if cb == 1:
                wch_free[0] = bank_w[1]
            tc_ = E.op("v", (lambda b_: lambda e: e.tensor_copy(
                out=raw_ext[:, 1:4], in_=tails[:, b_ * 3: b_ * 3 + 3]))(blkid),
                waits=evs + [tok.get("tailw")], sig=True)
            ct0 = E.op("v", (lambda b_: lambda e: e.tensor_scalar(
                out=acc_cv[:], in0=raw_ext[:, 4:4 + TH],
                scalar1=cconv[:, b_ * 4 + 3: b_ * 4 + 4], scalar2=None,
                op0=ALU.mult))(blkid), waits=evs + [tok.get("silp")],
                sig=True)
            prevt = ct0
            for k in range(3):
                prevt = E.op("v", (lambda b_, k_: lambda e:
                                   e.scalar_tensor_tensor(
                    out=acc_cv[:], in0=raw_ext[:, k_ + 1: k_ + 1 + TH],
                    scalar=cconv[:, b_ * 4 + k_: b_ * 4 + k_ + 1],
                    in1=acc_cv[:], op0=ALU.mult, op1=ALU.add))(blkid, k),
                    waits=[tc_, prevt], sig=True)
            cv = E.op("v", (lambda b_: lambda e: e.tensor_copy(
                out=tails[:, b_ * 3: b_ * 3 + 3],
                in_=raw_ext[:, 1 + TH: 4 + TH]))(blkid), waits=evs + [tc_],
                sig=True)
            raw_free = cv
            tok["tailw"] = cv
            th_ = E.op("s", lambda e: e.activation(out=ubuf[:], in_=acc_cv[:],
                                                   func=AF.Tanh, scale=0.5,
                                                   bias=zcol[:]),
                       waits=[cv, prevt], sig=True)
            sil = E.op("v", (lambda d_: lambda e: e.scalar_tensor_tensor(
                out=d_[:], in0=ubuf[:], scalar=1.0, in1=acc_cv[:],
                op0=ALU.add, op1=ALU.mult))(dstbuf),
                waits=[th_, prevt], sig=True)
            tok["silp"] = sil
            tok[f"bc{cb}"] = sil
        tok["raw_free"] = raw_free

        dtevs = []
        for tg in range(2):
            pb = tg
            mm = None
            for k in range(KD):
                mm = E.op("t", (lambda k_, tg_, s_, p_: lambda e: e.matmul(
                    out=PS[tg_][0:64, 0:512],
                    lhsT=WCH[1][:, k_ * 256: k_ * 256 + 64],
                    rhs=xnT[:, k_ * TH + tg_ * 512: k_ * TH + tg_ * 512 + 512],
                    start=s_, stop=p_))(k, tg, k == 0, k == KD - 1),
                    waits=[XR, ldt, bank_r[pb]] if k == 0 else (),
                    sig=(k == KD - 1))
            bank_w[pb] = mm
            ev = E.op("s", (lambda tg_: lambda e: e.activation(
                out=dtF[:, tg_ * 512: tg_ * 512 + 512], in_=PS[tg_][0:64, 0:512],
                func=AF.Identity, scale=1.0, bias=cdtb[:]))(tg),
                waits=[mm], sig=True)
            dtevs.append(ev)
            bank_r[pb] = ev
        wch_free[1] = bank_w[1]

        ab = E.op("s", lambda e: e.activation(
            out=logdA[:], in_=dtF[:], func=AF.Abs, scale=1.0,
            bias=zcol[0:64, :]), waits=dtevs + [NEA, tok.get("scan_done")],
            sig=True)
        E.op("s", lambda e: e.activation(out=cum_ht[:], in_=logdA[:],
                                         func=AF.Exp, scale=-1.0,
                                         bias=zcol[0:64, :]), waits=[ab])
        ln_ = E.op("s", lambda e: e.activation(out=logdA[:], in_=cum_ht[:],
                                               func=AF.Ln, scale=1.0,
                                               bias=onecol[0:64, :]), sig=True)
        mx_ = E.op("v", lambda e: e.tensor_scalar(out=cum_ht[:], in0=dtF[:],
                                                  scalar1=0.0, scalar2=None,
                                                  op0=ALU.max),
                   waits=[ln_], sig=True)
        ad_ = E.op("v", lambda e: e.tensor_add(out=dtF[:], in0=cum_ht[:],
                                               in1=logdA[:]),
                   waits=[mx_, ln_], sig=True)
        ld_ = E.op("v", lambda e: e.tensor_scalar(out=logdA[:], in0=dtF[:],
                                                  scalar1=negEA[:],
                                                  scalar2=None,
                                                  op0=ALU.mult),
                   waits=[ad_, NEA], sig=True)
        CUM = E.op("v", lambda e: e.tensor_tensor_scan(
            out=cum_ht[:], data0=cSCM[:], data1=logdA[:], initial=0.0,
            op0=ALU.mult, op1=ALU.add), waits=[ld_], sig=True)

        gm = None
        sg = tok.get("scan_guard", [])
        for c in range(NCH):
            E.op("t", (lambda c_: lambda e: e.matmul(
                out=PS[2][:, 0:64], lhsT=cum_ht[:, c_ * Q:(c_ + 1) * Q],
                rhs=cI64[:], start=True, stop=True))(c),
                waits=[CUM, bank_r[2]] + (sg if c == 0 else []))
            E.op("t", (lambda c_: lambda e: e.matmul(
                out=PS[2][:, 64:128], lhsT=dtF[:, c_ * Q:(c_ + 1) * Q],
                rhs=cI64[:], start=True, stop=True))(c))
            m3 = E.op("t", (lambda c_: lambda e: e.matmul(
                out=PS[2][:, 128:192], lhsT=logdA[:, c_ * Q:(c_ + 1) * Q],
                rhs=cI64[:], start=True, stop=True))(c), sig=True)
            bank_w[2] = m3
            e1 = E.op("v", (lambda c_: lambda e: e.tensor_copy(
                out=cumT[:, c_ * 64:(c_ + 1) * 64], in_=PS[2][:, 0:64]))(c),
                waits=[m3] + (sg if c == 0 else []), sig=True)
            dth_ = E.op("v", (lambda c_: lambda e: e.tensor_scalar(
                out=dtTh[:, c_ * 64:(c_ + 1) * 64], in0=PS[2][:, 64:128],
                scalar1=0.5, scalar2=None, op0=ALU.mult))(c),
                waits=[m3], sig=True)
            rc = E.op("v", lambda e: e.reciprocal(out=sm64c[:],
                                                  in_=PS[2][:, 64:128]),
                      waits=[m3], sig=True)
            E.op("v", (lambda c_: lambda e: e.tensor_mul(
                out=DdtB[:, c_ * 64:(c_ + 1) * 64], in0=sm64c[:],
                in1=cDrep[:]))(c), waits=[CDR, rc])
            e2 = E.op("v", lambda e: e.tensor_copy(out=sm64b[:],
                                                   in_=PS[2][:, 128:192]),
                      waits=[m3], sig=True)
            bank_r[2] = e2
            E.op("t", lambda e: e.matmul(out=PS[3][:, 0:64], lhsT=cSLT[:],
                                         rhs=sm64b[:], start=True, stop=True),
                 waits=[e2, bank_r[3]])
            s2_ = E.op("t", (lambda c_: lambda e: e.matmul(
                out=PS[3][:, 64:128], lhsT=cOR127[:],
                rhs=cumT[:, c_ * 64:(c_ + 1) * 64], start=True, stop=True))(c),
                waits=[e1] + (sg if c == 0 else []), sig=True)
            bank_w[3] = s2_
            bt = E.op("s", lambda e: e.activation(out=bTbuf[:],
                                                  in_=PS[3][:, 0:64],
                                                  func=AF.Exp, scale=1.0,
                                                  bias=zcol[:]),
                      waits=[s2_], sig=True)
            at = E.op("s", (lambda c_: lambda e: e.activation(
                out=AtotR[:, c_ * 64:(c_ + 1) * 64], in_=PS[3][:, 64:128],
                func=AF.Exp, scale=1.0, bias=zcol[:]))(c), sig=True)
            bank_r[3] = at
            tok["bcd_at"] = at
            E.op("v", (lambda c_: lambda e: e.tensor_mul(
                out=dtbh[:, c_ * 64:(c_ + 1) * 64],
                in0=dtTh[:, c_ * 64:(c_ + 1) * 64], in1=bTbuf[:]))(c),
                waits=[bt, dth_])
            b1 = E.op("t", (lambda c_: lambda e: e.matmul(
                out=PS[5][:, 0:128], lhsT=Bbf[:, c_ * Q:(c_ + 1) * Q],
                rhs=cIDbf[:], start=True, stop=True))(c),
                waits=[tok["bc0"], bank_r[5]] + (sg if c == 0 else []),
                sig=True)
            bank_w[5] = b1
            g1 = E.op("t", (lambda c_: lambda e: e.matmul(
                out=PS[4][:, 0:128], lhsT=Bbf[:, c_ * Q:(c_ + 1) * Q],
                rhs=Cbf[:, c_ * Q:(c_ + 1) * Q], start=True, stop=True))(c),
                waits=[tok["bc1"], bank_r[4]] + (sg if c == 0 else []),
                sig=True)
            bank_w[4] = g1
            bt2 = E.op("s", (lambda c_: lambda e: e.activation(
                out=BTt[:, c_ * 128:(c_ + 1) * 128], in_=PS[5][:, 0:128],
                func=AF.Identity, scale=0.5, bias=zcol[:]))(c),
                waits=[b1], sig=True)
            bank_r[5] = bt2
            gm = E.op("v", (lambda c_: lambda e: e.tensor_mul(
                out=Gm[:, c_ * 128:(c_ + 1) * 128], in0=PS[4][:, 0:128],
                in1=cUT025[:]))(c), waits=[g1], sig=True)
            bank_r[4] = gm
        tok["bcd_done"] = gm

    def phase_z(half):
        base = half * TH
        XR = tok["xnT_ready"]
        szfree = [None, None]
        ubfree = [None, None]
        g = -1
        cached = half != halves[0]
        for zc in range(32):
            lt = load_wchunk(256 * zc, 256, zc % 2, cached)
            for tb in range(NTH):
                g += 1
                pb = g % 2
                sg = g % 2
                mm = None
                for k in range(KD):
                    mm = E.op("t", (lambda b_, k_, tb_, pb_, s_, p_:
                                    lambda e: e.matmul(
                        out=PS[pb_][:, 0:256],
                        lhsT=xnT[:, k_ * TH + tb_ * 128:
                                 k_ * TH + tb_ * 128 + 128],
                        rhs=WCH[b_][:, k_ * 256:(k_ + 1) * 256],
                        start=s_, stop=p_))(zc % 2, k, tb, pb,
                                            k == 0, k == KD - 1),
                        waits=[XR, lt, bank_r[pb]] if k == 0 else (),
                        sig=(k == KD - 1))
                bank_w[pb] = mm
                if tb == NTH - 1:
                    wch_free[zc % 2] = mm
                th_ = E.op("s", (lambda pb_, sg_: lambda e: e.activation(
                    out=ubuf[:, sg_ * 256: sg_ * 256 + 256],
                    in_=PS[pb_][:, 0:256], func=AF.Tanh, scale=0.5,
                    bias=zcol[:]))(pb, sg),
                    waits=[mm, ubfree[sg]], sig=True)
                st = E.op("v", (lambda pb_, sg_: lambda e:
                                e.scalar_tensor_tensor(
                    out=szstg[sg_][:], in0=ubuf[:, sg_ * 256: sg_ * 256 + 256],
                    scalar=1.0, in1=PS[pb_][:, 0:256], op0=ALU.add,
                    op1=ALU.mult))(pb, sg),
                    waits=[th_, szfree[sg]], sig=True)
                bank_r[pb] = st
                ubfree[sg] = st
                szfree[sg] = E.op("y", (lambda r0, c0, sg_: lambda e:
                                        e.dma_start(
                    out=dsz.ap()[r0:r0 + 128, c0:c0 + 256],
                    in_=szstg[sg_][:]))(base + tb * 128, zc * 256, sg),
                    waits=[st], sig=True, dma=True, dma_sem=f"szst{sg}")
        tok["z_done"] = list(szfree)

    def phase_scan(half):
        XR = tok["xnT_ready"]
        cached = half != halves[0]
        pv = tok.setdefault("scan_pv", [{}, {}])
        last_proj = None
        assert max_heads % 2 == 0
        for p in range(max_heads // 2):
            wb = p % 2
            lt = load_wchunk(DSSM + 256 * p, 256, wb, cached)
            sz_ld = [None, None]
            sil = [None, None]
            for par in (0, 1):
                h = 2 * p + par
                srcv = dsz.ap()[half * TH:(half + 1) * TH,
                                h * 128:(h + 1) * 128].rearrange(
                    "(c p) w -> p c w", p=128)
                dstv = szt[par][:].rearrange("p (c w) -> p c w", c=NCH)
                zdw = tok.get("z_done") if do_z else [None, None]
                sz_ld[par] = E.op("y", (lambda s_, d_: lambda e: e.dma_start(
                    out=d_, in_=s_))(srcv, dstv),
                    waits=[pv[par].get("yg"), zdw[0], zdw[1]],
                    sig=True, dma=True, dma_sem=f"szld{par}")
                ptoks = []
                for tg in range(2):
                    mm = None
                    for k in range(KD):
                        mm = E.op("t", (lambda k_, sl, tg_, s_, p_, wb_:
                                        lambda e: e.matmul(
                            out=PS[tg_][:, 0:512],
                            lhsT=WCH[wb_][:, k_ * 256 + sl * 128:
                                          k_ * 256 + sl * 128 + 128],
                            rhs=xnT[:, k_ * TH + tg_ * 512:
                                    k_ * TH + tg_ * 512 + 512],
                            start=s_, stop=p_))(k, par, tg, k == 0,
                                                k == KD - 1, wb),
                            waits=[XR, lt, bank_r[tg]] if k == 0 else (),
                            sig=(k == KD - 1))
                    ptoks.append(mm)
                    bank_w[tg] = mm
                last_proj = ptoks[-1]
                if par == 1:
                    wch_free[wb] = ptoks[-1]
                evs = []
                for tg in range(2):
                    ev = E.op("s", (lambda tg_: lambda e: e.copy(
                        out=raw_ext[:, 4 + tg_ * 512: 4 + tg_ * 512 + 512],
                        in_=PS[tg_][:, 0:512]))(tg),
                        waits=[ptoks[tg], tok.get("raw_free")], sig=True)
                    evs.append(ev)
                    bank_r[tg] = ev
                tc_ = E.op("v", (lambda b_: lambda e: e.tensor_copy(
                    out=raw_ext[:, 1:4],
                    in_=tails[:, b_ * 3: b_ * 3 + 3]))(h),
                    waits=evs + [tok.get("tailw")], sig=True)
                ct0 = E.op("v", (lambda b_: lambda e: e.tensor_scalar(
                    out=acc_cv[:], in0=raw_ext[:, 4:4 + TH],
                    scalar1=cconv[:, b_ * 4 + 3: b_ * 4 + 4], scalar2=None,
                    op0=ALU.mult))(h), waits=evs + [tok.get("silp")],
                    sig=True)
                prevt = ct0
                for k in range(3):
                    prevt = E.op("v", (lambda b_, k_: lambda e:
                                       e.scalar_tensor_tensor(
                        out=acc_cv[:], in0=raw_ext[:, k_ + 1: k_ + 1 + TH],
                        scalar=cconv[:, b_ * 4 + k_: b_ * 4 + k_ + 1],
                        in1=acc_cv[:], op0=ALU.mult, op1=ALU.add))(h, k),
                        waits=[tc_, prevt], sig=True)
                cv = E.op("v", (lambda b_: lambda e: e.tensor_copy(
                    out=tails[:, b_ * 3: b_ * 3 + 3],
                    in_=raw_ext[:, 1 + TH: 4 + TH]))(h),
                    waits=evs + [tc_], sig=True)
                tok["raw_free"] = cv
                tok["tailw"] = cv
                th_ = E.op("s", lambda e: e.activation(
                    out=ubuf[:], in_=acc_cv[:], func=AF.Tanh, scale=0.5,
                    bias=zcol[:]), waits=[cv, prevt], sig=True)
                sil[par] = E.op("v", (lambda par_: lambda e:
                                      e.scalar_tensor_tensor(
                    out=xhbf[par_][:], in0=ubuf[:], scalar=1.0, in1=acc_cv[:],
                    op0=ALU.add, op1=ALU.mult))(par),
                    waits=[th_, prevt, pv[par].get("x1")], sig=True)
                tok["silp"] = sil[par]

            def emit_SX(c):
                for par in (0, 1):
                    h = 2 * p + par
                    cc = c * 64 + h
                    sl_ = c % 2
                    pb2 = 2 + par
                    x1_ = E.op("t", (lambda c_, par_: lambda e: e.matmul(
                        out=PS[2 + par_][:, 0:128],
                        lhsT=xhbf[par_][:, c_ * Q:(c_ + 1) * Q],
                        rhs=cIDbf[:], start=True, stop=True))(c, par),
                        waits=[sil[par], pv[par].get("dt1p"),
                               pv[par].get("dt2p"), pv[par].get("mk"),
                               pv[par].get("abp")], sig=True)
                    pv[par]["x1"] = x1_
                for par in (0, 1):
                    h = 2 * p + par
                    cc = c * 64 + h
                    cb_ = E.op("s", (lambda cc_, par_: lambda e: e.activation(
                        out=Mske[par_][:], in_=cIDf[:], func=AF.Identity,
                        scale=0.0, bias=cumT[:, cc_:cc_ + 1]))(cc, par),
                        waits=[tok["bcd_done"], pv[par].get("mk")], sig=True)
                    pv[par]["cb"] = cb_
                for par in (0, 1):
                    sl_ = c % 2
                    ct_ = E.op("t", (lambda par_: lambda e: e.matmul(
                        out=PS[2 + par_][:, 128:256],
                        lhsT=Mske[par_][:], rhs=cIDf[:],
                        start=True, stop=True))(par),
                        waits=[pv[par]["cb"]], sig=True)
                    pv[par]["ct"] = ct_
                for par in (0, 1):
                    h = 2 * p + par
                    cc = c * 64 + h
                    sl_ = c % 2
                    mk = E.op("v", (lambda cc_, par_: lambda e:
                                    e.scalar_tensor_tensor(
                        out=Mske[par_][:],
                        in0=PS[2 + par_][:, 128:256],
                        scalar=cumT[:, cc_:cc_ + 1], in1=cMASK[:],
                        op0=ALU.subtract, op1=ALU.add))(cc, par),
                        waits=[pv[par]["ct"], tok["bcd_done"]], sig=True)
                    pv[par]["mk"] = mk
                for par in (0, 1):
                    sl_ = c % 2
                    lt_ = E.op("s", (lambda par_: lambda e: e.activation(
                        out=LTb[par_][:], in_=Mske[par_][:], func=AF.Exp,
                        scale=1.0, bias=zcol[:]))(par),
                        waits=[pv[par]["mk"]], sig=True)
                    ab_ = E.op("s", (lambda par_: lambda e: e.activation(
                        out=aBrd[par_][:],
                        in_=PS[2 + par_][:, 128:256],
                        func=AF.Exp, scale=1.0, bias=ln05col[:]))(par),
                        sig=True)
                    pv[par]["lt"] = lt_
                    pv[par]["abp"] = ab_
                for par in (0, 1):
                    sl_ = c % 2
                    st_ = E.op("v", (lambda c_, par_, s_: lambda e:
                                     e.tensor_mul(
                        out=STb[par_][s_][:], in0=LTb[par_][:],
                        in1=Gm[:, c_ * 128:(c_ + 1) * 128]))(c, par, sl_),
                        waits=[pv[par]["lt"], pv[par].get(f"y2_{sl_}"),
                               tok["bcd_done"]], sig=True)
                    ac_ = E.op("v", (lambda c_, par_, s_: lambda e:
                                     e.tensor_mul(
                        out=aCb[par_][s_][:], in0=aBrd[par_][:],
                        in1=Cbf[:, c_ * Q:(c_ + 1) * Q]))(c, par, sl_),
                        waits=[pv[par]["abp"], tok["bc1"]], sig=True)
                    pv[par][f"st_{sl_}"] = st_
                    pv[par][f"acb_{sl_}"] = ac_
                for par in (0, 1):
                    h = 2 * p + par
                    cc = c * 64 + h
                    sl_ = c % 2
                    dt1 = E.op("v", (lambda cc_, par_, s_: lambda e:
                                     e.tensor_scalar(
                        out=dtxT[par_][s_][:],
                        in0=PS[2 + par_][:, 0:128],
                        scalar1=dtTh[:, cc_:cc_ + 1], scalar2=None,
                        op0=ALU.mult))(cc, par, sl_),
                        waits=[pv[par]["x1"], tok["bcd_done"],
                               pv[par].get(f"y2_{sl_}"),
                               pv[par].get(f"yd_{sl_}")], sig=True)
                    dt2 = E.op("v", (lambda cc_, par_, s_: lambda e:
                                     e.tensor_scalar(
                        out=DtXb[par_][s_][:],
                        in0=PS[2 + par_][:, 0:128],
                        scalar1=dtbh[:, cc_:cc_ + 1], scalar2=None,
                        op0=ALU.mult))(cc, par, sl_),
                        waits=[pv[par].get(f"d1_{sl_}"), tok["bcd_done"]],
                        sig=True)
                    pv[par]["dt1p"] = dt1
                    pv[par]["dt2p"] = dt2
                    pv[par][f"dt1_{sl_}"] = dt1
                    pv[par][f"dt2_{sl_}"] = dt2

            def emit_Y(c):
                for par in (0, 1):
                    h = 2 * p + par
                    sl_ = c % 2
                    E.op("t", (lambda h_, par_, s_: lambda e: e.matmul(
                        out=PS[4 + par_][:, 0:128], lhsT=aCb[par_][s_][:],
                        rhs=Hbf[:, h_ * 128:(h_ + 1) * 128], start=True,
                        stop=False))(h, par, sl_),
                        waits=[pv[par][f"acb_{sl_}"], pv[par].get("hu"),
                               HINIT, pv[par].get("ye")])
                    y2 = E.op("t", (lambda par_, s_: lambda e: e.matmul(
                        out=PS[4 + par_][:, 0:128], lhsT=STb[par_][s_][:],
                        rhs=dtxT[par_][s_][:], start=False, stop=True))(par, sl_),
                        waits=[pv[par][f"st_{sl_}"], pv[par][f"dt1_{sl_}"]],
                        sig=True)
                    pv[par][f"y2_{sl_}"] = y2
                    pv[par]["y2"] = y2
                    d1 = E.op("t", (lambda c_, par_, s_: lambda e: e.matmul(
                        out=PS[6 + par_][:, 0:128],
                        lhsT=BTt[:, c_ * 128:(c_ + 1) * 128],
                        rhs=DtXb[par_][s_][:], start=True, stop=True))(c, par, sl_),
                        waits=[pv[par][f"dt2_{sl_}"], pv[par].get("hu")],
                        sig=True)
                    pv[par][f"d1_{sl_}"] = d1
                    pv[par]["d1"] = d1
                for par in (0, 1):
                    h = 2 * p + par
                    cc = c * 64 + h
                    hu = E.op("v", (lambda h_, cc_, par_: lambda e:
                                    e.scalar_tensor_tensor(
                        out=Hbf[:, h_ * 128:(h_ + 1) * 128],
                        in0=Hbf[:, h_ * 128:(h_ + 1) * 128],
                        scalar=AtotR[:, cc_:cc_ + 1],
                        in1=PS[6 + par_][:, 0:128],
                        op0=ALU.mult, op1=ALU.add))(h, cc, par),
                        waits=[pv[par]["d1"], pv[par].get("hu"),
                               tok["bcd_at"]], sig=True)
                    pv[par]["hu"] = hu
                for par in (0, 1):
                    h = 2 * p + par
                    cc = c * 64 + h
                    sl_ = c % 2
                    yd = E.op("v", (lambda cc_, par_, s_: lambda e:
                                    e.scalar_tensor_tensor(
                        out=yDb[par_][:], in0=dtxT[par_][s_][:],
                        scalar=DdtB[:, cc_:cc_ + 1],
                        in1=PS[4 + par_][:, 0:128],
                        op0=ALU.mult, op1=ALU.add))(cc, par, sl_),
                        waits=[pv[par][f"y2_{sl_}"], tok["bcd_done"],
                               pv[par].get("yg")], sig=True)
                    pv[par][f"yd_{sl_}"] = yd
                    yg = E.op("v", (lambda c_, par_: lambda e: e.tensor_mul(
                        out=ygS[par_][:], in0=yDb[par_][:],
                        in1=szt[par_][:, c_ * 128:(c_ + 1) * 128]))(c, par),
                        waits=[sz_ld[par], yd, pv[par].get("yt")], sig=True)
                    pv[par]["yg"] = yg
                for par in (0, 1):
                    h = 2 * p + par
                    E.op("s", (lambda par_, col_: lambda e: e.activation(
                        out=dump[:, 0:128], in_=ygS[par_][:],
                        func=AF.Square,
                        accum_out=rmsb[:, col_:col_ + 1]))(
                        par, (half * NCH + c) * 64 + h),
                        waits=[pv[par]["yg"], RMSI])
                for par in (0, 1):
                    yt = E.op("t", (lambda par_: lambda e: e.matmul(
                        out=PS[4 + par_][:, 128:256], lhsT=ygS[par_][:],
                        rhs=cIDbf[:], start=True, stop=True))(par),
                        waits=[pv[par]["yg"]], sig=True)
                    pv[par]["yt"] = yt
                for par in (0, 1):
                    h = 2 * p + par
                    ye = E.op("s", (lambda par_, h_: lambda e: e.activation(
                        out=ygTS[par_][:], in_=PS[4 + par_][:, 128:256],
                        func=AF.Identity, scale=cnwT[:, h_:h_ + 1],
                        bias=zcol[:]))(par, h),
                        waits=[pv[par]["yt"], pv[par].get("ygst")], sig=True)
                    pv[par]["ye"] = ye
                    r0 = ((half * NCH + c) * 64 + h) * 128
                    stt = E.op("y", (lambda par_, r0_: lambda e: e.dma_start(
                        out=dygT.ap()[r0_:r0_ + 128, :],
                        in_=ygTS[par_][:]))(par, r0),
                        waits=[ye], sig=True, dma=True,
                        dma_sem=f"ygst{par}")
                    pv[par]["ygst"] = stt

            emit_SX(0)
            for c in range(NCH):
                if c + 1 < NCH:
                    emit_SX(c + 1)
                emit_Y(c)
        tok["xnT_free"] = last_proj
        tok["ygts_last"] = [pv[0].get("ygst"), pv[1].get("ygst")]
        tok["scan_guard"] = [pv[q].get(k) for q in (0, 1)
                             for k in ("dt1p", "dt2p", "abp", "mk",
                                       "yd_0", "yd_1", "ye", "hu")]

    for half in halves:
        phase_ln1(half)
        phase_bcd(half)
        if do_z:
            phase_z(half)
        phase_scan(half)
    if debug:
        dbg_bufs = [("s1c", s1c), ("s2c", s2c), ("t1c", t1c), ("t2c", t2c),
                    ("rstdc", rstdc), ("nmrc", nmrc),
                    ("Bbf", Bbf), ("Cbf", Cbf), ("BTt", BTt), ("Gm", Gm),
                    ("cumT", cumT), ("dtTh", dtTh), ("dtbh", dtbh),
                    ("DdtB", DdtB), ("AtotR", AtotR), ("dtF", dtF),
                    ("logdA", logdA), ("cum_ht", cum_ht),
                    ("xhbf0", xhbf[0]), ("xhbf1", xhbf[1]),
                    ("acc_cv", acc_cv), ("Hbf", Hbf)]
        dwaits = [tok["ygts_last"][0], tok["ygts_last"][1],
                  tok["scan_pv"][0].get("hu"), tok["scan_pv"][1].get("hu"),
                  tok.get("bcd_done")]
        for nm_, buf in dbg_bufs:
            dt_ = nc.dram_tensor("dbg_" + nm_, list(buf[:].shape),
                                 buf[:].dtype, kind="ExternalOutput")
            E.op("y", (lambda b_, d_: lambda e: e.dma_start(
                out=d_.ap(), in_=b_[:]))(buf, dt_), waits=dwaits,
                sig=True, dma=True, dma_sem="dbg")
    E.flush()

    scan_tail = [tok["ygts_last"][0], tok["ygts_last"][1],
                 tok["scan_pv"][0].get("hu"), tok["scan_pv"][1].get("hu")]

    if not do_passb:
        E.op("y", lambda e: e.nop(),
             waits=scan_tail[:2] + [tok.get("ln_prev")], force=True)
        E.flush()
        return nc

    E.new_group()
    RSC = None
    for tb in range(NT):
        rr = E.op("v", (lambda tb_: lambda e: e.reduce_sum(
            out=t1c[:], in_=rmsb[:, tb_ * 64:(tb_ + 1) * 64], axis=AX.X))(tb),
            waits=scan_tail + [RSC], sig=True)
        lg2 = E.op("s", lambda e: e.activation(out=t2c[:], in_=t1c[:],
                                               func=AF.Ln, scale=1.0 / DSSM,
                                               bias=eps4col[:]),
                   waits=[rr], sig=True)
        RSC = E.op("s", (lambda tb_: lambda e: e.activation(
            out=rscale[:, tb_:tb_ + 1], in_=t2c[:], func=AF.Exp, scale=-0.5,
            bias=zcol[:]))(tb), waits=[lg2], sig=True, force=True)
    E.flush()
    ga.close()

    gb = ExitStack()
    Wpan = sb(gb, "Wpan", [128, 64 * 1024], BF16)
    ygp = [sb(gb, f"ygp{i}", [128, 64 * 128], BF16) for i in range(2)]
    xres = [sb(gb, f"xres{i}", [128, 1024], F32) for i in range(2)]
    x1stg = [sb(gb, f"x1stg{i}", [128, 512], F32) for i in range(2)]
    dumpB = sb(gb, "dumpB", [128, 512], BF16)

    x1_store = {}

    def phase_passb():
        GATE = [RSC, scan_tail[0], scan_tail[1]]
        panel_cons = {}
        for g in range(4):
            subl = {}
            for j in range(8):
                src = dWout.ap()[j * 1024:(j + 1) * 1024,
                                 g * 1024:(g + 1) * 1024].rearrange(
                    "(k p) c -> p k c", p=128)
                dstv = Wpan[:, j * 8 * 1024:(j + 1) * 8 * 1024].rearrange(
                    "p (k c) -> p k c", k=8)
                subl[j] = E.op("g", (lambda s_, d_: lambda e: e.dma_start(
                    out=d_, in_=s_))(src, dstv),
                    waits=GATE + [panel_cons.get(g - 1)], sig=True, dma=True,
                    dma_sem="wout")
            for tb in range(NT):
                ypl = None
                for jj in range(8):
                    src = dygT.ap()[(tb * 64 + jj * 8) * 128:
                                    (tb * 64 + jj * 8 + 8) * 128, :].rearrange(
                        "(k p) c -> p k c", p=128)
                    dstv = ygp[tb % 2][:, jj * 8 * 128:(jj + 1) * 8 * 128] \
                        .rearrange("p (k c) -> p k c", k=8)
                    ypl = E.op("y", (lambda s_, d_: lambda e: e.dma_start(
                        out=d_, in_=s_))(src, dstv),
                        waits=[scan_tail[0], scan_tail[1],
                               tok.get(f"ygp_free{tb % 2}")],
                        sig=True, dma=True, dma_sem=f"ygp{tb % 2}")
                xr = E.op("y", (lambda tb_, g_, i_: lambda e: e.dma_start(
                    out=xres[i_][:],
                    in_=dx.ap()[tb_ * 128:(tb_ + 1) * 128,
                                g_ * 1024:(g_ + 1) * 1024]))(tb, g, tb % 2),
                    waits=[tok.get(f"xres_free{tb % 2}")], sig=True, dma=True,
                    dma_sem=f"xres{tb % 2}")
                for ds in range(2):
                    pb = (tb * 2 + ds) % 4
                    mm = None
                    for k in range(64):
                        if k == 0:
                            w = [ypl, bank_r[pb], subl[7]]
                        else:
                            w = ()
                        mm = E.op("t", (lambda tb_, k_, ds_, pb_, s_, p_:
                                        lambda e: e.matmul(
                            out=PS[pb_][:, 0:512],
                            lhsT=ygp[tb_ % 2][:, k_ * 128:(k_ + 1) * 128],
                            rhs=Wpan[:, k_ * 1024 + ds_ * 512:
                                     k_ * 1024 + ds_ * 512 + 512],
                            start=s_, stop=p_))(tb, k, ds, pb, k == 0, k == 63),
                            waits=w, sig=(k == 63))
                    bank_w[pb] = mm
                    if ds == 1:
                        tok[f"ygp_free{tb % 2}"] = mm
                        panel_cons[g] = mm
                    ev = E.op("v", (lambda tb_, ds_, pb_, i_: lambda e:
                                    e.scalar_tensor_tensor(
                        out=x1stg[i_][:], in0=PS[pb_][:, 0:512],
                        scalar=rscale[:, tb_:tb_ + 1],
                        in1=xres[tb_ % 2][:, ds_ * 512: ds_ * 512 + 512],
                        op0=ALU.mult, op1=ALU.add))(tb, ds, pb, ds),
                        waits=[mm, xr, RSC, tok.get(f"x1s_free{ds}")],
                        sig=True)
                    bank_r[pb] = ev
                    if ds == 1:
                        tok[f"xres_free{tb % 2}"] = ev
                    r1 = E.op("v", (lambda i_: lambda e: e.reduce_sum(
                        out=t1c[:], in_=x1stg[i_][:], axis=AX.X))(ds),
                        waits=[ev, tok.get("r1t")], sig=True)
                    sa = E.op("v", (lambda tb_: lambda e: e.tensor_add(
                        out=stats[:, tb_ * 2:tb_ * 2 + 1],
                        in0=stats[:, tb_ * 2:tb_ * 2 + 1], in1=t1c[:]))(tb),
                        waits=[r1, tok.get(f"stat_{tb}"), VINIT], sig=True)
                    tok[f"stat_{tb}"] = sa
                    tok["r1t"] = sa
                    sq = E.op("s", (lambda i_: lambda e: e.activation(
                        out=dumpB[:], in_=x1stg[i_][:], func=AF.Square,
                        accum_out=t2c[:]))(ds), waits=[ev, tok.get("a2t")],
                        sig=True)
                    a2 = E.op("v", (lambda tb_: lambda e: e.tensor_add(
                        out=stats[:, tb_ * 2 + 1:tb_ * 2 + 2],
                        in0=stats[:, tb_ * 2 + 1:tb_ * 2 + 2], in1=t2c[:]))(tb),
                        waits=[sq, tok.get(f"stat2_{tb}"), VINIT], sig=True)
                    tok[f"stat2_{tb}"] = a2
                    tok["a2t"] = a2
                    st = E.op("y", (lambda tb_, g_, ds_, i_: lambda e:
                                    e.dma_start(
                        out=dx1.ap()[tb_ * 128:(tb_ + 1) * 128,
                                     g_ * 1024 + ds_ * 512:
                                     g_ * 1024 + ds_ * 512 + 512],
                        in_=x1stg[i_][:]))(tb, g, ds, ds),
                        waits=[ev, a2], sig=True, dma=True,
                        dma_sem=f"x1s{ds}")
                    tok[f"x1s_free{ds}"] = st
                    x1_store[(g, tb, ds)] = st
        prev = None
        prev_store = tok.get("xns_last")
        for tb in range(NT):
            lx = E.op("y", (lambda tb_: lambda e: e.dma_start(
                out=xtile[:], in_=dx1.ap()[tb_ * 128:(tb_ + 1) * 128, :]))(tb),
                waits=[x1_store[(3, NT - 1, 1)], x1_store[(3, NT - 1, 0)],
                       prev], sig=True, dma=True, dma_sem="xt")
            m1t = E.op("v", (lambda tb_: lambda e: e.tensor_scalar(
                out=t1c[:], in0=stats[:, tb_ * 2:tb_ * 2 + 1],
                scalar1=1.0 / D, scalar2=None, op0=ALU.mult))(tb),
                waits=[x1_store[(3, tb, 1)], tok.get(f"stat_{tb}"),
                       tok.get(f"stat2_{tb}")], sig=True)
            m2t = E.op("v", (lambda tb_: lambda e: e.tensor_scalar(
                out=t2c[:], in0=stats[:, tb_ * 2 + 1:tb_ * 2 + 2],
                scalar1=1.0 / D, scalar2=None, op0=ALU.mult))(tb),
                waits=[tok.get(f"stat2_{tb}")], sig=True)
            nv = E.op("v", lambda e: e.scalar_tensor_tensor(
                out=s1c[:], in0=t1c[:], scalar=t1c[:], in1=t2c[:],
                op0=ALU.mult, op1=ALU.subtract), waits=[m1t, m2t], sig=True)
            lg = E.op("s", lambda e: e.activation(out=t2c[:], in_=s1c[:],
                                                  func=AF.Ln, scale=-1.0,
                                                  bias=epscol[:]),
                      waits=[nv], sig=True)
            rst = E.op("s", lambda e: e.activation(out=rstdc[:], in_=t2c[:],
                                                   func=AF.Exp, scale=-0.5,
                                                   bias=zcol[:]),
                      waits=[lg], sig=True, force=True)
            p1t = E.op("v", lambda e: e.tensor_scalar(
                out=t2c[:], in0=t1c[:], scalar1=rstdc[:], scalar2=None,
                op0=ALU.mult), waits=[rst, m1t], sig=True)
            nm = E.op("v", lambda e: e.tensor_scalar(
                out=nmrc[:], in0=t2c[:], scalar1=-1.0, scalar2=None,
                op0=ALU.mult), waits=[p1t], sig=True)
            norm = E.op("s", lambda e: e.activation(
                out=xnst[:], in_=xtile[:], func=AF.Identity, scale=rstdc[:],
                bias=nmrc[:]), waits=[nm, lx, rst, prev_store], sig=True,
                force=True)
            prev = E.op("y", (lambda tb_: lambda e: e.dma_start(
                out=dxn2.ap()[tb_ * 128:(tb_ + 1) * 128, :], in_=xnst[:]))(tb),
                waits=[norm], sig=True, dma=True, dma_sem="xns")
            prev_store = prev
            tok[f"xn2_store{tb}"] = prev
        tok["passb_done"] = prev

    phase_passb()
    E.flush()
    gb.close()

    if not do_ffn:
        E.op("y", lambda e: e.nop(), waits=[tok["passb_done"]], force=True)
        E.flush()
        return nc

    gc_ = ExitStack()
    AwT = sb(gc_, "AwT", [128, KD * 512], BF16)
    BwT = sb(gc_, "BwT", [128, 4 * 4096], BF16)
    WqT = sb(gc_, "WqT", [128, KD * 16], BF16)
    keysT = sb(gc_, "keysT", [16, 64], BF16)
    Asb = sb(gc_, "Asb", [128, 4096], BF16)
    Bsb = sb(gc_, "Bsb", [128, 4 * 512], BF16)
    Wqsb = sb(gc_, "Wqsb", [16, 1024], BF16)
    keysb = sb(gc_, "keysb", [64, 16], BF16)
    xn2p = sb(gc_, "xn2p", [128, KD * 512], BF16)
    zT = sb(gc_, "zT", [128, 4 * 2048], BF16)
    qT = sb(gc_, "qT", [16, 2048], BF16)
    gateT = sb(gc_, "gateT", [64, NT * 128], BF16)
    sctile = sb(gc_, "sctile", [128, 64], F32)
    wk1 = sb(gc_, "wk1", [128, 64], F32)
    wk2 = sb(gc_, "wk2", [128, 64], F32)
    mx8 = sb(gc_, "mx8", [128, 8], F32)
    mx8b = sb(gc_, "mx8b", [128, 8], F32)
    gtile = sb(gc_, "gtile", [128, 64], F32)
    cEXP8 = sb(gc_, "cEXP8", [64, 512], BF16)
    x1b = [sb(gc_, f"x1b{i}", [128, D], F32) for i in range(2)]
    outstg = [sb(gc_, f"outstg{i}", [128, 512], F32) for i in range(2)]

    def phase_ffn():
        PBD = tok["passb_done"]
        E8L = E.op("y", lambda e: e.dma_start(out=cEXP8[:], in_=dEXP8.ap()),
                   waits=[PBD], sig=True, dma=True, dma_sem="e8l")
        atok = None
        for rg in range(4):
            la = E.op("g", (lambda rg_: lambda e: e.dma_start(
                out=Asb[:], in_=dAw.ap()[rg_ * 128:(rg_ + 1) * 128, :]))(rg),
                waits=[PBD, tok.get("Asb_free")], sig=True, dma=True,
                dma_sem="asb")
            for k in range(KD):
                m = E.op("t", (lambda k_: lambda e: e.matmul(
                    out=PS[6][:, 0:128], lhsT=Asb[:, k_ * 128:(k_ + 1) * 128],
                    rhs=cIDbf[:], start=True, stop=True))(k),
                    waits=[la, bank_r[6]], sig=True)
                bank_w[6] = m
                atok = E.op("v", (lambda k_, rg_: lambda e: e.tensor_copy(
                    out=AwT[:, k_ * 512 + rg_ * 128:
                            k_ * 512 + rg_ * 128 + 128],
                    in_=PS[6][:, 0:128]))(k, rg), waits=[m], sig=True)
                bank_r[6] = atok
            tok["Asb_free"] = atok
        btok = None
        for rg in range(8):
            lb = E.op("g", (lambda rg_: lambda e: e.dma_start(
                out=Bsb[:].rearrange("p (k c) -> p k c", k=4),
                in_=dBw.ap()[rg_ * 512:(rg_ + 1) * 512, :].rearrange(
                    "(k p) c -> p k c", p=128)))(rg),
                waits=[PBD, tok.get("Bsb_free")], sig=True, dma=True,
                dma_sem="bsb")
            for ko in range(4):
                for kn in range(4):
                    m = E.op("t", (lambda ko_, kn_: lambda e: e.matmul(
                        out=PS[7][:, 0:128],
                        lhsT=Bsb[:, ko_ * 512 + kn_ * 128:
                                 ko_ * 512 + kn_ * 128 + 128],
                        rhs=cIDbf[:], start=True, stop=True))(ko, kn),
                        waits=[lb, bank_r[7]], sig=True)
                    bank_w[7] = m
                    btok = E.op("v", (lambda rg_, ko_, kn_: lambda e:
                                      e.tensor_copy(
                        out=BwT[:, kn_ * 4096 + rg_ * 512 + ko_ * 128:
                                kn_ * 4096 + rg_ * 512 + ko_ * 128 + 128],
                        in_=PS[7][:, 0:128]))(rg, ko, kn), waits=[m],
                        sig=True)
                    bank_r[7] = btok
            tok["Bsb_free"] = btok
        lk = E.op("g", lambda e: e.dma_start(out=keysb[:], in_=dkeys.ap()),
                  waits=[PBD], sig=True, dma=True, dma_sem="keysl")
        qts = None
        for quarter in range(4):
            lq = E.op("g", (lambda q_: lambda e: e.dma_start(
                out=Wqsb[:], in_=dWq.ap()[:, q_ * 1024:(q_ + 1) * 1024]))(quarter),
                waits=[PBD, qts], sig=True, dma=True, dma_sem="wq")
            for kk in range(8):
                k = quarter * 8 + kk
                m = E.op("t", (lambda kk_: lambda e: e.matmul(
                    out=PS[6][:, 128 + kk_ * 16: 128 + kk_ * 16 + 16],
                    lhsT=Wqsb[:, kk_ * 128:(kk_ + 1) * 128], rhs=cI16[:],
                    start=True, stop=True))(kk),
                    waits=[lq] + ([bank_r[6]] if kk == 0 else []), sig=True)
                bank_w[6] = m
                if kk == 7:
                    qts = E.op("v", (lambda k_: lambda e: e.tensor_copy(
                        out=WqT[:, (k_ - 7) * 16:(k_ + 1) * 16],
                        in_=PS[6][:, 128:256]))(k), waits=[m], sig=True)
                    bank_r[6] = qts
        mk_ = E.op("t", lambda e: e.matmul(
            out=PS[7][0:16, 128:192], lhsT=keysb[:], rhs=cI64bf[:],
            start=True, stop=True), waits=[lk, bank_r[7]], sig=True)
        bank_w[7] = mk_
        kev = E.op("v", lambda e: e.tensor_copy(out=keysT[:],
                                                in_=PS[7][0:16, 128:192]),
                   waits=[mk_], sig=True)
        bank_r[7] = kev

        ztoks = {}
        qtoks = {}
        for tg in range(4):
            pl = None
            for k in range(KD):
                pl = E.op("s", (lambda tg_, k_: lambda e: e.dma_start(
                    out=xn2p[:, k_ * 512:(k_ + 1) * 512],
                    in_=dxn2.ap()[tg_ * 512:(tg_ + 1) * 512,
                                  k_ * 128:(k_ + 1) * 128],
                    transpose=True))(tg, k),
                    waits=[tok["passb_done"],
                           tok.get("xn2p_free")], sig=True, dma=True,
                    dma_sem="xn2p")
            mm = None
            for k in range(KD):
                mm = E.op("t", (lambda k_, s_, p_: lambda e: e.matmul(
                    out=PS[4][0:16, 0:512], lhsT=WqT[:, k_ * 16:(k_ + 1) * 16],
                    rhs=xn2p[:, k_ * 512:(k_ + 1) * 512], start=s_,
                    stop=p_))(k, k == 0, k == KD - 1),
                    waits=[pl, qts, bank_r[4]] if k == 0 else (),
                    sig=(k == KD - 1))
            bank_w[4] = mm
            qe = E.op("v", (lambda tg_: lambda e: e.tensor_copy(
                out=qT[:, tg_ * 512:(tg_ + 1) * 512],
                in_=PS[4][0:16, 0:512]))(tg), waits=[mm], sig=True)
            bank_r[4] = qe
            qtoks[tg] = qe
            for nb in range(4):
                pb = nb % 2
                mm = None
                for k in range(KD):
                    mm = E.op("t", (lambda k_, nb_, pb_, s_, p_: lambda e:
                                    e.matmul(
                        out=PS[pb_][:, 0:512],
                        lhsT=AwT[:, k_ * 512 + nb_ * 128:
                                 k_ * 512 + nb_ * 128 + 128],
                        rhs=xn2p[:, k_ * 512:(k_ + 1) * 512], start=s_,
                        stop=p_))(k, nb, pb, k == 0, k == KD - 1),
                        waits=[pl, atok, bank_r[pb]] if k == 0 else (),
                        sig=(k == KD - 1))
                bank_w[pb] = mm
                ze = E.op("v", (lambda nb_, tg_, pb_: lambda e: e.tensor_copy(
                    out=zT[:, nb_ * 2048 + tg_ * 512:
                           nb_ * 2048 + tg_ * 512 + 512],
                    in_=PS[pb_][:, 0:512]))(nb, tg, pb), waits=[mm], sig=True)
                bank_r[pb] = ze
                ztoks[(tg, nb)] = ze
                if nb == 3:
                    tok["xn2p_free"] = mm

        gttoks = {}
        for tb in range(NT):
            mm = E.op("t", (lambda tb_: lambda e: e.matmul(
                out=PS[4][:, 0:64], lhsT=qT[:, tb_ * 128:(tb_ + 1) * 128],
                rhs=keysT[:], start=True, stop=True))(tb),
                waits=[qtoks[tb // 4], kev, bank_r[4]], sig=True)
            bank_w[4] = mm
            sc = E.op("v", lambda e: e.tensor_copy(out=sctile[:],
                                                   in_=PS[4][:, 0:64]),
                      waits=[mm], sig=True)
            bank_r[4] = sc
            a = E.op("v", lambda e: e.max(out=mx8[:], in_=sctile[:]),
                     waits=[sc], sig=True)
            b = E.op("v", lambda e: e.match_replace(
                out=wk1[:], in_to_replace=mx8[:], in_values=sctile[:],
                imm_value=-1e30), waits=[a], sig=True, force=True)
            c = E.op("v", lambda e: e.max(out=mx8b[:], in_=wk1[:]),
                     waits=[b], sig=True, force=True)
            d = E.op("v", lambda e: e.match_replace(
                out=wk2[:], in_to_replace=mx8b[:], in_values=wk1[:],
                imm_value=-1e30), waits=[c], sig=True, force=True)
            nm1 = E.op("v", lambda e: e.tensor_scalar(
                out=t1c[:], in0=mx8[:, 0:1], scalar1=-1.0, scalar2=None,
                op0=ALU.mult), waits=[d], sig=True, force=True)
            ex = E.op("s", lambda e: e.activation(
                out=wk1[:], in_=sctile[:], func=AF.Exp, scale=1.0,
                bias=t1c[:]), waits=[nm1], sig=True)
            msk = E.op("v", lambda e: e.tensor_tensor(
                out=wk2[:], in0=sctile[:], in1=wk2[:], op=ALU.not_equal),
                waits=[ex], sig=True, force=True)
            gmm = E.op("v", lambda e: e.tensor_mul(out=wk1[:], in0=wk1[:],
                                                   in1=wk2[:]),
                       waits=[msk], sig=True, force=True)
            dn = E.op("v", lambda e: e.reduce_sum(out=t2c[:], in_=wk1[:],
                                                  axis=AX.X),
                      waits=[gmm], sig=True, force=True)
            rd = E.op("v", lambda e: e.reciprocal(out=t1c[:], in_=t2c[:]),
                      waits=[dn], sig=True, force=True)
            gt = E.op("v", lambda e: e.tensor_scalar(
                out=gtile[:], in0=wk1[:], scalar1=t1c[:], scalar2=None,
                op0=ALU.mult), waits=[rd], sig=True, force=True)
            gm2 = E.op("t", lambda e: e.matmul(
                out=PS[5][0:64, 0:128], lhsT=gtile[:], rhs=cIDf[:],
                start=True, stop=True), waits=[gt, bank_r[5]], sig=True)
            bank_w[5] = gm2
            ge = E.op("v", (lambda tb_: lambda e: e.tensor_copy(
                out=gateT[:, tb_ * 128:(tb_ + 1) * 128],
                in_=PS[5][0:64, 0:128]))(tb), waits=[gm2], sig=True)
            bank_r[5] = ge
            gttoks[tb] = ge

        zgtoks = {}
        for tg in range(4):
            for nb in range(4):
                pb = 2 + (nb % 2)
                mm = E.op("t", (lambda tg_, nb_, pb_: lambda e: e.matmul(
                    out=PS[pb_][:, 0:512],
                    lhsT=cEXP8[:, nb_ * 128:(nb_ + 1) * 128],
                    rhs=gateT[:, tg_ * 512:(tg_ + 1) * 512], start=True,
                    stop=True))(tg, nb, pb),
                    waits=[gttoks[tg * 4 + 3], E8L, bank_r[pb]], sig=True)
                bank_w[pb] = mm
                zg = E.op("v", (lambda tg_, nb_, pb_: lambda e: e.tensor_mul(
                    out=zT[:, nb_ * 2048 + tg_ * 512:
                           nb_ * 2048 + tg_ * 512 + 512],
                    in0=zT[:, nb_ * 2048 + tg_ * 512:
                           nb_ * 2048 + tg_ * 512 + 512],
                    in1=PS[pb_][:, 0:512]))(tg, nb, pb),
                    waits=[mm, ztoks[(tg, nb)]], sig=True)
                bank_r[pb] = zg
                zgtoks[(tg, nb)] = zg
        ZGL = zgtoks[(3, 3)]

        out_last = None
        for tb in range(NT):
            xl = E.op("y", (lambda tb_: lambda e: e.dma_start(
                out=x1b[tb_ % 2][:],
                in_=dx1.ap()[tb_ * 128:(tb_ + 1) * 128, :]))(tb),
                waits=[x1_store[(3, NT - 1, 1)], x1_store[(3, NT - 1, 0)],
                       tok.get(f"x1b_free{tb % 2}")],
                sig=True, dma=True, dma_sem=f"x1b{tb % 2}")
            for og in range(8):
                pb = og % 2
                mm = None
                for kn in range(4):
                    mm = E.op("t", (lambda tb_, og_, kn_, pb_, s_, p_:
                                    lambda e: e.matmul(
                        out=PS[pb_][:, 0:512],
                        lhsT=zT[:, kn_ * 2048 + tb_ * 128:
                                kn_ * 2048 + tb_ * 128 + 128],
                        rhs=BwT[:, kn_ * 4096 + og_ * 512:
                                kn_ * 4096 + og_ * 512 + 512],
                        start=s_, stop=p_))(tb, og, kn, pb, kn == 0, kn == 3),
                        waits=[ZGL, btok, bank_r[pb]] if kn == 0 else (),
                        sig=(kn == 3))
                bank_w[pb] = mm
                fv = E.op("v", (lambda tb_, og_, pb_, i_: lambda e:
                                e.scalar_tensor_tensor(
                    out=outstg[i_][:], in0=PS[pb_][:, 0:512], scalar=2.0,
                    in1=x1b[tb_ % 2][:, og_ * 512: og_ * 512 + 512],
                    op0=ALU.mult, op1=ALU.add))(tb, og, pb, og % 2),
                    waits=[mm, xl, tok.get(f"outst_free{og % 2}")], sig=True)
                bank_r[pb] = fv
                if og == 7:
                    tok[f"x1b_free{tb % 2}"] = fv
                out_last = E.op("y", (lambda tb_, og_, i_: lambda e:
                                      e.dma_start(
                    out=dout.ap()[tb_ * 128:(tb_ + 1) * 128,
                                  og_ * 512: og_ * 512 + 512],
                    in_=outstg[i_][:]))(tb, og, og % 2),
                    waits=[fv], sig=True, dma=True,
                    dma_sem=f"outst{og % 2}")
                tok[f"outst_free{og % 2}"] = out_last
        E.op("y", lambda e: e.nop(),
             waits=[tok["outst_free0"], tok["outst_free1"]], force=True)

    phase_ffn()
    if debug:
        for nm_, buf in [("rscale", rscale), ("stats", stats),
                         ("gateT", gateT), ("qT", qT), ("zT", zT),
                         ("sctile", sctile), ("gtile", gtile), ("wk1", wk1),
                         ("wk2", wk2), ("mx8", mx8), ("mx8b", mx8b)]:
            dt_ = nc.dram_tensor("dbgf_" + nm_, list(buf[:].shape),
                                 buf[:].dtype, kind="ExternalOutput")
            E.op("y", (lambda b_, d_: lambda e: e.dma_start(
                out=d_.ap(), in_=b_[:]))(buf, dt_),
                waits=[tok.get("outst_free0"), tok.get("outst_free1")],
                sig=True, dma=True, dma_sem="dbg")
    E.flush()
    gc_.close()
    return nc


from concourse.bass_utils import run_bass_kernel_spmd

_BUILT = None

def kernel(**inputs):
    global _BUILT
    in_maps = [host_arrays(inputs, core) for core in range(8)]
    if _BUILT is None:
        _BUILT = build(debug=False)
    nc = _BUILT
    res = run_bass_kernel_spmd(nc, in_maps, core_ids=list(range(8)))
    out = np.stack([np.asarray(res.results[i]["out"]) for i in range(8)], axis=0)
    return out.astype(np.float32)
